# version 21
# baseline (speedup 1.0000x reference)
import numpy as np

import concourse.bass as bass
import concourse.mybir as mybir
from concourse.bass_utils import run_bass_kernel_spmd

F32 = mybir.dt.float32
BF16 = mybir.dt.bfloat16
FP8 = mybir.dt.float8e4
AF = mybir.ActivationFunctionType
ALU = mybir.AluOpType
AX = mybir.AxisListType

D = 2048
E = 8192
H = 16
CH = 64
TSPAN = 2048
P = 512
NT = TSPAN // P
KC = D // 128
NG = P // 128
ESLAB = 1024
NSLAB = E // ESLAB
QSCALE = 128.0 ** -0.5
DEBUG = False
G16 = 1.0 / 16.0


def _t(a):
    return np.ascontiguousarray(np.asarray(a, np.float32).T)


def _mpke(wt):
    return np.ascontiguousarray(
        wt.reshape(KC, 128, 16, 128).transpose(2, 1, 0, 3).reshape(16, 128, D))


def host_prep(inputs):
    x = np.asarray(inputs["L0_attn_x"], np.float32)
    wq = _mpke(np.ascontiguousarray(np.asarray(inputs["L0_attn_Wq"], np.float32)))
    wk = _mpke(np.ascontiguousarray(np.asarray(inputs["L0_attn_Wk"], np.float32)))
    wv = _mpke(np.ascontiguousarray(np.asarray(inputs["L0_attn_Wv"], np.float32)))
    wgt = _mpke(np.ascontiguousarray(np.asarray(inputs["L0_attn_Wgate"], np.float32)))
    wo = _mpke(np.ascontiguousarray(np.asarray(inputs["L0_attn_Wo"], np.float32)))
    wg1 = np.asarray(inputs["L0_attn_Wg1"], np.float32)
    wg1t = np.ascontiguousarray(
        wg1.reshape(KC, 128, 16).transpose(1, 0, 2).reshape(128, KC * 16))
    wg2a = np.ascontiguousarray(np.concatenate(
        [np.asarray(inputs["L0_attn_Wg2"], np.float32),
         np.asarray(inputs["L0_attn_bg2"], np.float32)[None, :]], axis=0))
    nw = np.ascontiguousarray(
        np.asarray(inputs["L0_attn_norm_w"], np.float32)[None, :])
    wupt = _t(inputs["L0_ffn_W_up"])
    wgat = _t(inputs["L0_ffn_W_gate"])
    wdnn = np.ascontiguousarray(np.asarray(inputs["L0_ffn_W_down"], np.float32))

    u2 = np.tril(np.ones((128, 128), np.float32)).T
    u2x = u2.copy()
    u2x[0, :] = 0.0
    v2 = (np.arange(128)[:, None] > np.arange(128)[None, :]).astype(np.float32)
    id128 = np.eye(128, dtype=np.float32)
    am = (np.arange(128)[:, None] <= np.arange(128)[None, :]).astype(np.float32)
    am8 = np.ascontiguousarray(np.tile(am, (1, 4)))

    in_maps = []
    for c in range(8):
        b, s = c // 4, c % 4
        xs = np.ascontiguousarray(x[b, s * TSPAN:(s + 1) * TSPAN, :].T)
        wcomb = np.zeros((8, 8), np.float32)
        mvec = np.zeros((8,), np.float32)
        for j in range(8):
            bj, sj = j // 4, j % 4
            if bj == b and sj < s:
                mvec[j] = 1.0
                for i in range(8):
                    bi, si = i // 4, i % 4
                    if bi == b and sj < si < s:
                        wcomb[j, i] = 1.0
        cmask = np.zeros((128, 72), np.float32)
        cmask[:, 0:64] = wcomb.reshape(1, 64)
        cmask[:, 64:72] = mvec.reshape(1, 8)
        wred = np.ascontiguousarray(np.stack([
            wupt[c * 256:c * 256 + 128], wupt[c * 256 + 128:c * 256 + 256],
            wgat[c * 256:c * 256 + 128], wgat[c * 256 + 128:c * 256 + 256],
            wdnn[c * 256:c * 256 + 128], wdnn[c * 256 + 128:c * 256 + 256],
        ]))
        in_maps.append({
            "x": xs, "wred": wred,
            "wq": wq, "wk": wk, "wv": wv, "wgt": wgt, "wo": wo,
            "wg1t": wg1t, "wg2a": wg2a, "nw": nw,
            "wupt": wupt, "wgat": wgat, "wdnn": wdnn,
            "u2": u2, "u2x": u2x, "v2": v2, "id128": id128, "am8": am8,
            "cmask": np.ascontiguousarray(cmask),
        })
    return in_maps


class Reg:
    __slots__ = ("w", "r")

    def __init__(self):
        self.w = None
        self.r = []

    def rd(self):
        return [self.w] if self.w else []

    def wr(self):
        out = list(self.r)
        if self.w:
            out.append(self.w)
        return out

    def set_w(self, tok):
        if tok is not None:
            self.w = tok
            self.r = []

    def add_r(self, tok):
        if tok is not None:
            self.r.append(tok)


class KB:
    ENGS = ("pe", "dv", "ac", "sy", "gp")

    def __init__(self):
        self.steps = {e: [] for e in self.ENGS}
        self.cnt = {e: 0 for e in ("pe", "dv", "ac", "sy", "gp", "cc")}
        self.sat = {e: {} for e in self.ENGS}
        self.sems = {}
        self.regs = {}

    def R(self, name):
        if name not in self.regs:
            self.regs[name] = Reg()
        return self.regs[name]

    def _wait(self, eng, tok):
        if tok is None:
            return
        sem, val = tok
        if val <= 0 or self.sat[eng].get(sem, 0) >= val:
            return
        self.sat[eng][sem] = val
        self.steps[eng].append(
            lambda e, s=sem, v=val: e.wait_ge(self.sems[s], v))

    def op(self, eng, fn, waits=(), tick=False):
        for t in waits:
            self._wait(eng, t)
        if tick:
            self.cnt[eng] += 1
            v = self.cnt[eng]

            def run(e, fn=fn, eng=eng):
                fn(e).then_inc(self.sems[eng], 1)
            self.steps[eng].append(run)
            return (eng, v)
        self.steps[eng].append(lambda e, fn=fn: fn(e))
        return None

    def dma(self, q, out_ap, in_ap, waits=(), sem=None, **kw):
        for t in waits:
            self._wait(q, t)
        s = sem or q
        self.cnt[s] = self.cnt.get(s, 0) + 16
        v = self.cnt[s]

        def run(e, o=out_ap, i=in_ap, s=s, kw=kw):
            e.dma_start(out=o, in_=i, **kw).then_inc(self.sems[s], 16)
        self.steps[q].append(run)
        return (s, v)

    def tok(self, eng):
        return (eng, self.cnt[eng])


def build():
    nc = bass.Bass("TRN2", target_bir_lowering=False)
    kb = KB()
    R = kb.R

    dt = nc.dram_tensor
    x_d = dt("x", [D, TSPAN], F32, kind="ExternalInput")
    y_d = dt("y", [D, TSPAN], F32, kind="ExternalOutput")
    wq_d = dt("wq", [16, 128, D], F32, kind="ExternalInput")
    wk_d = dt("wk", [16, 128, D], F32, kind="ExternalInput")
    wv_d = dt("wv", [16, 128, D], F32, kind="ExternalInput")
    wgt_d = dt("wgt", [16, 128, D], F32, kind="ExternalInput")
    wo_d = dt("wo", [16, 128, D], F32, kind="ExternalInput")
    wg1_d = dt("wg1t", [128, 256], F32, kind="ExternalInput")
    wg2_d = dt("wg2a", [17, D], F32, kind="ExternalInput")
    nw_d = dt("nw", [1, 128], F32, kind="ExternalInput")
    wupt_d = dt("wupt", [D, E], F32, kind="ExternalInput")
    wgat_d = dt("wgat", [D, E], F32, kind="ExternalInput")
    wdnn_d = dt("wdnn", [D, E], F32, kind="ExternalInput")
    u2_d = dt("u2", [128, 128], F32, kind="ExternalInput")
    u2x_d = dt("u2x", [128, 128], F32, kind="ExternalInput")
    v2_d = dt("v2", [128, 128], F32, kind="ExternalInput")
    id_d = dt("id128", [128, 128], F32, kind="ExternalInput")
    am8_d = dt("am8", [128, 512], F32, kind="ExternalInput")
    cm_d = dt("cmask", [128, 72], F32, kind="ExternalInput")
    wqb_d = dt("wqb", [16, 128, D], BF16)
    wkb_d = dt("wkb", [16, 128, D], BF16)
    wvb_d = dt("wvb", [16, 128, D], BF16)
    wgtb_d = dt("wgtb", [16, 128, D], BF16)
    wob_d = dt("wob", [16, 128, D], BF16)
    t8up_d = dt("t8up", [NSLAB, 128, KC * ESLAB], FP8)
    t8ga_d = dt("t8ga", [NSLAB, 128, KC * ESLAB], FP8)
    t8dn_d = dt("t8dn", [16, 128, E], FP8)
    x1_d = dt("x1s", [KC, 128, TSPAN], F32)
    xn_d = dt("xns", [KC, 128, TSPAN], BF16)
    k_d = dt("ks", [H, 128, TSPAN], BF16)
    v_d = dt("vs", [TSPAN // 128, 128, D], BF16)
    gk_d = dt("gks", [TSPAN // 128, 128, D], F32)
    ccin_d = dt("ccin", [128, 2064], F32)
    ccout_d = dt("ccout", [1024, 2064], F32)
    wred_d = dt("wred", [6, 128, 8192], F32, kind="ExternalInput")
    rin_d = dt("rin", [128, 4], F32)
    rout_d = dt("rout", [128, 4], F32)

    st = nc.sbuf_tensor
    from contextlib import ExitStack
    with ExitStack() as ctx:
        en = ctx.enter_context
        AR = en(st("AR", [128, 90112], BF16))
        S_f = en(st("S_f", [128, H * 128], F32))
        S_b = en(st("S_b", [128, H * 128], BF16))
        Gtot = en(st("Gtot", [128, H], F32))
        cm_t = en(st("cm_t", [128, 72], F32))
        u2_t = en(st("u2_t", [128, 128], F32))
        u2x_t = en(st("u2x_t", [128, 128], F32))
        v2_t = en(st("v2_t", [128, 128], F32))
        id_t = en(st("id_t", [128, 128], BF16))
        am8_t = en(st("am8_t", [128, 512], BF16))
        wg1_t = en(st("wg1_t", [128, 256], BF16))
        wg2_t = en(st("wg2_t", [17, D], BF16))
        nw_t = en(st("nw_t", [1, 128], BF16))
        onesb = en(st("onesb", [128, 1], BF16))
        onesr = en(st("onesr", [1, 128], F32))
        rows = en(st("rows", [1, 1536], F32))
        rows2 = en(st("rows2", [1, 2048], BF16))
        sc = en(st("sc", [128, 16], F32))
        accv = en(st("accv", [128, 4], F32))
        gred = en(st("gred", [128, 16], F32))
        egl_f = en(st("egl_f", [128, 32], F32))
        constt = en(st("constt", [128, 2], F32))
        pBig = en(nc.psum_tensor("pBig", [128, 2048], F32))
        pO = en(nc.psum_tensor("pO", [128, 1024], F32))
        pC = en(nc.psum_tensor("pC", [128, 512], F32))
        pD = en(nc.psum_tensor("pD", [128, 1024], BF16))
        s_pe = en(nc.semaphore("s_pe"))
        s_dv = en(nc.semaphore("s_dv"))
        s_ac = en(nc.semaphore("s_ac"))
        s_sy = en(nc.semaphore("s_sy"))
        s_gp = en(nc.semaphore("s_gp"))
        s_cc = en(nc.semaphore("s_cc"))
        extra_sems = {}
        for nm in ("wa0", "wa1", "wa2", "wa3", "wu0", "wu1", "wgx", "wd0",
                   "wd1", "vc0", "vc1", "gkq0", "gkq1", "ws0", "ws1",
                   "cbk", "cbv", "cbq", "cbg", "cbo", "t80", "t81",
                   "t84", "xp0"):
            extra_sems[nm] = en(nc.semaphore("s_" + nm))
        block = en(nc.Block())
        kb.sems = {"pe": s_pe, "dv": s_dv, "ac": s_ac, "sy": s_sy,
                   "gp": s_gp, "cc": s_cc}
        kb.sems.update(extra_sems)

        ar_f = AR.ap().bitcast(F32)
        ar_b = AR.ap()
        ar_8 = AR.ap().bitcast(FP8)

        def fv(off, p, n):
            return ar_f[0:p, off // 4: off // 4 + n]

        def bv(off, p, n):
            return ar_b[0:p, off // 2: off // 2 + n]

        def v8(off, p, n):
            return ar_8[0:p, off: off + n]

        A0, A1, A2, A3, A4, A5 = (0, 32768, 49152, 114688, 147456, 163840)
        x_t = fv(A0, 128, 8192)
        q_v = bv(A0, 128, 8192)
        gs_v = bv(A0 + 16384, 128, 8192)
        xn_t = bv(A1, 128, 8192)
        ltb_v = fv(A1, 128, 4096)
        h_v = bv(A2, 128, 32768)
        wsl = [fv(A2, 128, 8192), fv(A2 + 32768, 128, 8192)]
        kF_v = bv(A2, 128, 8192)
        vAll_v = bv(A2 + 16384, 128, 8192)
        og_v = bv(A2 + 16384, 128, 8192)
        zb1_v = fv(A2 + 32768, 128, 2048)
        zb2_v = fv(A2 + 40960, 128, 2048)
        z1_v = bv(A2 + 49152, 32, 512)
        t1_v = fv(A2 + 50176, 128, 512)
        KtilA_v = bv(A2 + 52224, 128, 2048)
        ATmA_v = bv(A2 + 56320, 128, 2048)
        KtilB_v = bv(A2 + 32768, 128, 2048)
        ATmB_v = bv(A2 + 36864, 128, 2048)
        gS_v = fv(A2 + 40960, 128, 2048)
        gG_v = fv(A2 + 49152, 128, 128)
        tj_v = fv(A2 + 50176, 128, 16)
        ej_v = fv(A2 + 50432, 128, 16)
        S0_v = fv(A2 + 53248, 128, 2048)
        wupb = [v8(A3, 128, KC * ESLAB), v8(A3 + 16384, 128, KC * ESLAB)]
        wab = [bv(A3, 128, 2048), bv(A3 + 4096, 128, 2048),
               bv(A3 + 8192, 128, 2048), bv(A3 + 12288, 128, 2048)]
        Eqg_v = bv(A3 + 8192, 128, 2048)
        EQh_v = bv(A3 + 12288, 128, 2048)
        EKh_v = bv(A3 + 16384, 128, 2048)
        EKs_v = bv(A3 + 20480, 128, 2048)
        vC_2 = [bv(A3 + 24576, 128, 2048), bv(A3 + 28672, 128, 2048)]
        wgab = v8(A4, 128, KC * ESLAB)
        wdnb = [v8(A4, 128, E), v8(A4 + 8192, 128, E)]
        qg_v = bv(A4, 128, 2048)
        Qh_v = bv(A4 + 4096, 128, 2048)
        Kh_v = bv(A4 + 8192, 128, 2048)
        o2_v = bv(A4 + 12288, 128, 2048)
        og1_v = bv(A1, 128, 2048)
        xb_v = bv(A4, 128, 8192)
        silu_v = fv(A5, 128, 4096)
        gkg_v = [fv(A5, 128, 2048), fv(A5 + 8192, 128, 2048)]
        t8o_v2 = [v8(A5, 128, 8192), v8(A5 + 8192, 128, 8192)]
        xsq_v = bv(A5, 128, 8192)

        pb = [R("pb%d" % i) for i in range(4)]
        po = [R("po%d" % i) for i in range(2)]
        pc_r = R("pc")
        pd_r = R("pd")

        def pbig_wr():
            return [t for b in pb for t in b.wr()]

        def po_wr():
            return [t for b in po for t in b.wr()]

        def mm(out, lhsT, rhs, start, stop, waits=(), tick=False):
            return kb.op("pe", lambda e: e.matmul(out, lhsT, rhs, start=start,
                                                  stop=stop), waits, tick)

        def act(out, in_, func, waits=(), tick=True, **kw):
            return kb.op("ac", lambda e: e.activation(out, in_, func, **kw),
                         waits, tick)

        def fence():
            return [kb.tok(e) for e in ("pe", "dv", "ac", "sy", "gp")]

        e3 = lambda t: t.rearrange("p (h l) -> p h l", h=H)

        dbg_seen = set()

        def dbg(name, ap, waits, reg=None):
            if not DEBUG or name in dbg_seen:
                return
            dbg_seen.add(name)
            sh = list(ap.shape)
            dd = dt("dbg_" + name, sh, ap.dtype, kind="ExternalOutput")
            t = kb.dma("sy", dd.ap(), ap, waits=list(waits))
            if reg is not None:
                reg.add_r(t)

        t_cm = kb.dma("gp", cm_t[:, :], cm_d[:, :])
        t_u2 = kb.dma("gp", u2_t[:, :], u2_d[:, :])
        t_u2x = kb.dma("gp", u2x_t[:, :], u2x_d[:, :])
        t_v2 = kb.dma("gp", v2_t[:, :], v2_d[:, :])
        t_id = kb.dma("gp", id_t[:, :], id_d[:, :])
        t_am = kb.dma("gp", am8_t[:, :], am8_d[:, :])
        t_wg1 = kb.dma("gp", wg1_t[:, :], wg1_d[:, :])
        t_wg2 = kb.dma("gp", wg2_t[:, :], wg2_d[:, :])
        t_nw = kb.dma("gp", nw_t[:, :], nw_d[:, :])
        kb.op("dv", lambda e: e.memset(constt[:, 0:1], 1e-5))
        kb.op("dv", lambda e: e.memset(constt[:, 1:2], 1.0))
        nc.const_aps.register(F32, 1e-5, constt[:, 0:1])
        nc.const_aps.register(F32, 1.0, constt[:, 1:2])
        kb.op("dv", lambda e: e.memset(onesb[:, :], 1.0))
        kb.op("dv", lambda e: e.memset(onesr[:, :], 1.0))
        xpre = None
        for c in range(KC):
            xpre = kb.dma("gp", x_t[:, c * 512:(c + 1) * 512],
                          x_d[c * 128:(c + 1) * 128, 0:512], sem="xp0")
        kb.op("dv", lambda e: e.memset(S_f[:, :], 0.0))
        kb.op("dv", lambda e: e.memset(S_b[:, :], 0.0))
        kb.op("dv", lambda e: e.memset(Gtot[:, :], 0.0))
        t_mems = kb.op("dv", lambda e: e.memset(accv[:, :], 0.0), tick=True)

        wslr = [R("wsl0"), R("wsl1")]

        def reduce_mat(dram, nslab, width, col):
            tlast = None
            for i in range(nslab):
                b = i % 2
                tl = kb.dma("sy", wsl[b][:, 0:width],
                            dram[i * 128:(i + 1) * 128, :],
                            waits=wslr[b].wr(), sem="ws%d" % b)
                wslr[b].set_w(tl)
                t_red = kb.op("dv", lambda e, b=b, w=width: e.reduce_sum(
                    sc[:, 15:16], wsl[b][:, 0:w], axis=AX.X,
                    apply_absolute_value=True), waits=[tl, t_mems], tick=True)
                tlast = kb.op("dv", lambda e, c=col: e.tensor_add(
                    accv[:, c:c + 1], accv[:, c:c + 1], sc[:, 15:16]),
                    waits=[t_red], tick=True)
                wslr[b].add_r(tlast)
            return tlast

        tlast = None
        for i in range(6):
            b = i % 2
            tl = kb.dma("sy", wsl[b][:, :], wred_d[i, :, :],
                        waits=wslr[b].wr(), sem="ws%d" % b)
            wslr[b].set_w(tl)
            t_red = kb.op("dv", lambda e, b=b: e.reduce_sum(
                sc[:, 15:16], wsl[b][:, :], axis=AX.X,
                apply_absolute_value=True), waits=[tl, t_mems], tick=True)
            tlast = kb.op("dv", lambda e, c=i // 2: e.tensor_add(
                accv[:, c:c + 1], accv[:, c:c + 1], sc[:, 15:16]),
                waits=[t_red], tick=True)
            wslr[b].add_r(tlast)
        t_rw = kb.dma("gp", rin_d[:, :], accv[:, :], waits=[tlast])
        kb.cnt["cc"] += 1
        rcv = kb.cnt["cc"]
        kb._wait("gp", t_rw)

        def run_rcc(e):
            e.collective_compute(
                "AllReduce", ALU.add, replica_groups=[list(range(8))],
                ins=[rin_d.ap().opt()], outs=[rout_d.ap().opt()],
            ).then_inc(kb.sems["cc"], 1)
        kb.steps["gp"].append(run_rcc)
        t_rcc = ("cc", rcv)
        t_rrd = kb.dma("gp", accv[:, :], rout_d[:, :], waits=[t_rcc])
        CT = {}
        onescol = sc[:, 14:15]
        t_of = kb.op("dv", lambda e: e.memset(onescol, 1.0), tick=True)
        t_sum = mm(pC[0:1, 8:12], onescol, accv[:, 0:4], True, True,
                   waits=[t_of, t_rrd] + pc_r.wr(), tick=True)
        pc_r.set_w(t_sum)
        t_s = kb.op("dv", lambda e: e.tensor_scalar(
            rows[0:1, 0:4], pC[0:1, 8:12], float(2.0 ** -24), 1e-8,
            ALU.mult, ALU.add), waits=[t_sum], tick=True)
        pc_r.add_r(t_s)
        t_bc = mm(pC[:, 12:16], onesr[:, :], rows[0:1, 0:4], True, True,
                  waits=[t_s] + pc_r.wr(), tick=True)
        pc_r.set_w(t_bc)
        t_scp = kb.op("dv", lambda e: e.tensor_copy(sc[:, 0:4], pC[:, 12:16]),
                      waits=[t_bc], tick=True)
        pc_r.add_r(t_scp)
        kb.op("dv", lambda e: e.tensor_scalar_mul(sc[:, 4:7], sc[:, 0:3], 0.5),
              waits=[t_scp])
        kb.op("dv", lambda e: e.tensor_scalar_mul(sc[:, 7:10], sc[:, 0:3], -0.5))
        t_g1 = kb.op("dv", lambda e: e.tensor_tensor(
            sc[:, 10:11], sc[:, 1:2], sc[:, 2:3], ALU.mult), tick=True)
        t_scal = kb.op("dv", lambda e: e.tensor_scalar_mul(
            sc[:, 10:11], sc[:, 10:11], 0.5), waits=[t_g1], tick=True)
        dbg("sc", sc[:, :], [t_scal])

        t8r = [R("t8o0"), R("t8o1")]

        def quant_mat(dram, t8dram, nslab, width, tc, mc, dn):
            for i in range(nslab):
                b = i % 2
                t8b = i % 2
                t8o = t8o_v2[t8b]
                tl = kb.dma("sy", wsl[b][:, 0:width],
                            dram[i * 128:(i + 1) * 128, :],
                            waits=wslr[b].wr(), sem="ws%d" % b)
                wslr[b].set_w(tl)
                t2 = None
                for hf in range(max(1, width // 4096)):
                    w0 = hf * 4096
                    wd = min(4096, width)
                    t_lt = kb.op("dv", lambda e, b=b, w0=w0, wd=wd, mc=mc:
                                 e.tensor_scalar(ltb_v[:, 0:wd],
                                                 wsl[b][:, w0:w0 + wd],
                                                 sc[:, mc:mc + 1], None,
                                                 ALU.is_lt),
                                 waits=[tl, t_scal] +
                                 (t8r[t8b].wr() if hf == 0 else []),
                                 tick=True)
                    t2 = kb.op("dv", lambda e, t8o=t8o, b=b, w0=w0, wd=wd,
                               tc=tc: e.scalar_tensor_tensor(
                                   t8o[:, w0:w0 + wd],
                                   wsl[b][:, w0:w0 + wd],
                                   sc[:, tc:tc + 1], ltb_v[:, 0:wd],
                                   ALU.is_gt, ALU.subtract), waits=[t_lt],
                               tick=True)
                    wslr[b].add_r(t2)
                t8r[t8b].set_w(t2)
                if not dn:
                    tw = kb.dma("sy",
                                t8dram[:, :, i * ESLAB:(i + 1) * ESLAB]
                                .rearrange("s p e -> p s e"),
                                t8o[:, 0:8192].rearrange(
                                    "p (s e) -> p s e", s=NSLAB),
                                waits=[t2], sem="t8%d" % t8b)
                    t8r[t8b].add_r(tw)
                else:
                    tw = kb.dma("sy", t8dram[:, :, i * 128:(i + 1) * 128]
                                .rearrange("m p e -> p m e"),
                                t8o[:, 0:2048].rearrange(
                                    "p (m e) -> p m e", m=16), waits=[t2])
                    t8r[t8b].add_r(tw)

        quant_mat(wupt_d, t8up_d, 16, 8192, 4, 7, False)
        G_UP = [("t80", kb.cnt.get("t80", 0)), ("t81", kb.cnt.get("t81", 0))]
        quant_mat(wgat_d, t8ga_d, 16, 8192, 5, 8, False)
        G_GA = [("t80", kb.cnt.get("t80", 0)), ("t81", kb.cnt.get("t81", 0))]

        tb16_2 = [bv(A5, 128, 4096), bv(A5 + 8192, 128, 4096)]
        stg8 = v8(A3, 128, 8192)
        stg_r = R("stg8")
        tb_rs = [R("tb16a"), R("tb16b")]
        for i in range(16):
            b = i % 2
            tl = kb.dma("sy", wsl[b][:, 0:8192],
                        wdnn_d[i * 128:(i + 1) * 128, :],
                        waits=wslr[b].wr(), sem="ws%d" % b)
            wslr[b].set_w(tl)
            for hf in range(2):
                w0 = hf * 4096
                tb16 = tb16_2[hf]
                tb_r = tb_rs[hf]
                t_lt = kb.op("dv", lambda e, b=b, w0=w0:
                             e.tensor_scalar(ltb_v[:, 0:4096],
                                             wsl[b][:, w0:w0 + 4096],
                                             sc[:, 9:10], None, ALU.is_lt),
                             waits=[tl, t_scal] + tb_r.wr(), tick=True)
                t2 = kb.op("dv", lambda e, tb16=tb16, b=b, w0=w0:
                           e.scalar_tensor_tensor(
                               tb16[:, :], wsl[b][:, w0:w0 + 4096],
                               sc[:, 6:7], ltb_v[:, 0:4096],
                               ALU.is_gt, ALU.subtract), waits=[t_lt],
                           tick=True)
                wslr[b].add_r(t2)
                tb_r.set_w(t2)
                for kg in range(8):
                    bk = kg % 2
                    for kk in range(4):
                        k = kg * 4 + kk
                        tmm = mm(pO[:, bk * 512 + kk * 128:
                                    bk * 512 + (kk + 1) * 128],
                                 tb16[:, k * 128:(k + 1) * 128], id_t[:, :],
                                 True, True,
                                 waits=(po[bk].wr() + [t2, t_id])
                                 if kk == 0 else (), tick=(kk == 3))
                    po[bk].set_w(tmm)
                    tb_r.add_r(tmm)
                    t_ev = act(stg8[:, (hf * 32 + kg * 4) * 128:
                                    (hf * 32 + kg * 4 + 4) * 128],
                               pO[:, bk * 512:(bk + 1) * 512], AF.Copy,
                               waits=[tmm] + (stg_r.wr()
                                              if (hf == 0 and kg == 0)
                                              else []))
                    po[bk].add_r(t_ev)
                    stg_r.set_w(t_ev) if (hf == 1 and kg == 7) else                         stg_r.add_r(t_ev)
            tw = kb.dma("sy", t8dn_d[i, :, :], stg8[:, :], waits=[t_ev],
                        sem="t84")
            stg_r.add_r(tw)
        G_DN = [("t84", kb.cnt.get("t84", 0))]
        t_q8 = kb.tok("sy")

        def emit_ln(pf):
            xr, xnr = R("x"), R("xn")
            tc = None
            w0 = list(pf) + xr.rd() + R("xb").wr()
            for c in range(KC):
                tc = kb.op("dv", lambda e, c=c: e.tensor_copy(
                    xb_v[:, c * 512:(c + 1) * 512],
                    x_t[:, c * 512:(c + 1) * 512]),
                    waits=(w0 if c == 0 else ()), tick=(c == KC - 1))
            R("xb").set_w(tc)
            tsq = kb.op("dv", lambda e: e.tensor_tensor(
                xsq_v[:, :], xb_v[:, :], xb_v[:, :], ALU.mult),
                waits=R("xsq").wr(), tick=True)
            R("xsq").set_w(tsq)
            for c in range(KC):
                mm(pC[0:1, 0:512], onesb[:, :], xb_v[:, c * 512:(c + 1) * 512],
                   c == 0, c == KC - 1, waits=(pc_r.wr() + [tc]) if c == 0
                   else ())
            for c in range(KC):
                tS = mm(pO[0:1, 0:512], onesb[:, :],
                        xsq_v[:, c * 512:(c + 1) * 512], c == 0, c == KC - 1,
                        waits=(po_wr() + [tsq]) if c == 0 else (),
                        tick=(c == KC - 1))
            pc_r.set_w(tS)
            po[0].set_w(tS)
            t_m = kb.op("dv", lambda e: e.tensor_scalar_mul(
                rows[0:1, 0:512], pC[0:1, 0:512], 1.0 / 2048.0), waits=[tS],
                tick=True)
            t_rd = kb.op("dv", lambda e: e.tensor_scalar_mul(
                rows[0:1, 512:1024], pO[0:1, 0:512], 1.0 / 2048.0), tick=True)
            pc_r.add_r(t_rd)
            po[0].add_r(t_rd)
            t_m2 = kb.op("dv", lambda e: e.tensor_tensor(
                rows[0:1, 1024:1536], rows[0:1, 0:512], rows[0:1, 0:512],
                ALU.mult), waits=[t_m], tick=True)
            t_var = kb.op("dv", lambda e: e.tensor_sub(
                rows[0:1, 512:1024], rows[0:1, 512:1024],
                rows[0:1, 1024:1536]), waits=[t_rd, t_m2], tick=True)
            t_ln = act(rows[0:1, 1024:1536], rows[0:1, 512:1024], AF.Ln,
                       waits=[t_var], bias=1e-5)
            t_r = act(rows[0:1, 512:1024], rows[0:1, 1024:1536], AF.Exp,
                      waits=[t_ln], scale=-0.5)
            t_mr = kb.op("dv", lambda e: e.tensor_tensor(
                rows[0:1, 1024:1536], rows[0:1, 0:512], rows[0:1, 512:1024],
                ALU.mult), waits=[t_r, t_m], tick=True)
            mm(pO[:, 0:512], onesr[:, :], rows[0:1, 512:1024], True, True,
               waits=po_wr() + [t_r, t_mr])
            t_bc2 = mm(pO[:, 512:1024], onesr[:, :], rows[0:1, 1024:1536],
                       True, True, tick=True)
            po[0].set_w(t_bc2)
            po[1].set_w(t_bc2)
            t_xn = None
            xn_toks = []
            for c in range(KC):
                t_t1 = kb.op("dv", lambda e, c=c: e.tensor_tensor(
                    t1_v[:, :], x_t[:, c * 512:(c + 1) * 512], pO[:, 0:512],
                    ALU.mult),
                    waits=(([t_bc2] + xnr.wr() + R("t1").wr()) if c == 0
                           else ()), tick=True)
                t_xn = kb.op("dv", lambda e, c=c: e.tensor_sub(
                    xn_t[:, c * 512:(c + 1) * 512], t1_v[:, :],
                    pO[:, 512:1024]), waits=[t_t1], tick=True)
                xn_toks.append(t_xn)
            po[0].add_r(t_xn)
            po[1].add_r(t_xn)
            xnr.set_w(t_xn)
            R("t1").set_w(t_xn)
            return t_xn, xn_toks, t_rd

        def emit_ffn(t_xn, pf, xn_toks=None, t_stats=None, q="sy",
                     gu=None, gg=None, gd=None, mid=None):
            xr = R("x")
            wu = [R("wu0"), R("wu1")]
            wg = R("wg")
            hr, sr = R("h"), R("silu")
            t_h = None
            for s in range(NSLAB):
                b = s % 2
                tlu = kb.dma(q, wupb[b][:, :], t8up_d[s, :, :],
                             waits=wu[b].wr() +
                             ((gu if gu is not None else [t_q8]) + list(pf)
                              if s < 2 else []),
                             sem="wu%d" % b)
                wu[b].set_w(tlu)
                tlg = kb.dma(q, wgab[:, :], t8ga_d[s, :, :],
                             waits=wg.wr() +
                             ((gg if gg is not None else [t_q8]) +
                              [t_stats or t_xn] + list(pf)
                              if s == 0 else []),
                             sem="wgx")
                wg.set_w(tlg)
                silu_toks = []
                for m in range(8):
                    bk = m % 4
                    psl = pBig[:, bk * 512:(bk + 1) * 512]
                    for k in range(KC):
                        xw = [xn_toks[k]] if xn_toks else [t_xn]
                        tmm = mm(psl, wupb[b][:, k * ESLAB + m * 128:
                                              k * ESLAB + (m + 1) * 128],
                                 xn_t[:, k * 512:(k + 1) * 512],
                                 k == 0, k == KC - 1,
                                 waits=(pb[bk].wr() + [tlu] + xw)
                                 if k == 0 else xw, tick=(k == KC - 1))
                    pb[bk].set_w(tmm)
                    wu[b].add_r(tmm)
                    tsl = act(silu_v[:, m * 512:(m + 1) * 512], psl, AF.Silu,
                              waits=[tmm, t_scal] + (sr.wr() if m == 0
                                                     else []), scale=sc[:, 0:1])
                    pb[bk].add_r(tsl)
                    silu_toks.append(tsl)
                sr.set_w(silu_toks[-1])
                for m in range(8):
                    bk = m % 2
                    psl = pO[:, bk * 512:(bk + 1) * 512]
                    for k in range(KC):
                        xw = [xn_toks[k]] if xn_toks else [t_xn]
                        tmm = mm(psl, wgab[:, k * ESLAB + m * 128:
                                          k * ESLAB + (m + 1) * 128],
                                 xn_t[:, k * 512:(k + 1) * 512],
                                 k == 0, k == KC - 1,
                                 waits=(po[bk].wr() + [tlg] + xw)
                                 if k == 0 else xw, tick=(k == KC - 1))
                    po[bk].set_w(tmm)
                    wg.add_r(tmm)
                    t_h = kb.op("dv", lambda e, s=s, m=m, psl=psl:
                                e.tensor_tensor(
                                    h_v[:, (s * 8 + m) * 512:
                                        (s * 8 + m + 1) * 512],
                                    silu_v[:, m * 512:(m + 1) * 512], psl,
                                    ALU.mult),
                                waits=[tmm, silu_toks[m]] +
                                (hr.wr() if (s == 0 and m == 0) else []),
                                tick=True)
                    po[bk].add_r(t_h)
                    sr.add_r(t_h)
            hr.set_w(t_h)
            dbg("h_t%d" % kb.cnt.get("_ffn", 0), h_v, [t_h], hr)
            kb.cnt["_ffn"] = kb.cnt.get("_ffn", 0) + 1
            wd = [R("wd0"), R("wd1")]
            t_out = None
            t_gate_done = kb.tok("pe")
            if mid is not None:
                mid()
            for m in range(16):
                b = m % 2
                tld = kb.dma(q, wdnb[b][:, :], t8dn_d[m, :, :],
                             waits=wd[b].wr() +
                             ((gd if gd is not None else [t_q8]) +
                              [t_gate_done, t_h] if m < 2 else []),
                             sem="wd%d" % b)
                wd[b].set_w(tld)
                bk = m % 4
                psl = pBig[:, bk * 512:(bk + 1) * 512]
                for k in range(E // 128):
                    tmm = mm(psl, wdnb[b][:, k * 128:(k + 1) * 128],
                             h_v[:, k * 512:(k + 1) * 512],
                             k == 0, k == E // 128 - 1,
                             waits=(pb[bk].wr() + [tld, t_h]) if k == 0
                             else (), tick=(k == E // 128 - 1))
                pb[bk].set_w(tmm)
                wd[b].add_r(tmm)
                t_out = kb.op("dv", lambda e, m=m, psl=psl:
                              e.scalar_tensor_tensor(
                                  x_t[:, m * 512:(m + 1) * 512], psl,
                                  sc[:, 10:11], x_t[:, m * 512:(m + 1) * 512],
                                  ALU.mult, ALU.add),
                              waits=[tmm] + (xr.wr() if m == 0 else []),
                              tick=True)
                pb[bk].add_r(t_out)
                hr.add_r(t_out)
            xr.set_w(t_out)
            return t_out

        war = [R("wa0"), R("wa1"), R("wa2"), R("wa3")]

        def emit_proj(wdram, t_xn, evac, pf, xn_toks=None):
            t_ev = None
            for m in range(16):
                b = m % 4
                tl = kb.dma("sy", wab[b][:, :], wdram[m, :, :],
                            waits=war[b].wr() + (list(pf) if m < 4 else []),
                            sem="wa%d" % b)
                war[b].set_w(tl)
                bk = m % 2
                psl = pO[:, bk * 512:(bk + 1) * 512]
                for k in range(KC):
                    xw = [xn_toks[k]] if xn_toks else [t_xn]
                    tmm = mm(psl, wab[b][:, k * 128:(k + 1) * 128],
                             xn_t[:, k * 512:(k + 1) * 512], k == 0,
                             k == KC - 1,
                             waits=(po[bk].wr() + [tl] + xw) if k == 0
                             else xw, tick=(k == KC - 1))
                po[bk].set_w(tmm)
                war[b].add_r(tmm)
                t_ev = evac(m, psl, tmm)
                po[bk].add_r(t_ev)
            return t_ev

        def emit_gk(tile, g, gb, t_xn, xn_toks=None):
            gkr = R("gkg%d" % gb)
            for k in range(KC):
                xw = [xn_toks[k]] if xn_toks else [t_xn]
                tz = mm(pC[0:16, 0:512], wg1_t[:, k * 16:(k + 1) * 16],
                        xn_t[:, k * 512:(k + 1) * 512], k == 0, k == KC - 1,
                        waits=(pc_r.wr() + [t_wg1] + xw) if k == 0 else xw,
                        tick=(k == KC - 1))
            pc_r.set_w(tz)
            tms = kb.op("dv", lambda e: e.memset(z1_v[0:32, :], 1.0),
                        waits=R("z1").wr(), tick=True)
            tcp = kb.op("dv", lambda e: e.tensor_copy(
                z1_v[0:16, :], pC[0:16, 0:512]), waits=[tz, tms], tick=True)
            R("z1").set_w(tcp)
            pc_r.add_r(tcp)
            t_gk = None
            for hf in range(2):
                for n in range(2):
                    bk = n
                    tzz = mm(pO[:, bk * 512:(bk + 1) * 512],
                             z1_v[0:17, g * 128:(g + 1) * 128],
                             wg2_t[0:17, (hf * 2 + n) * 512:
                                   (hf * 2 + n + 1) * 512], True, True,
                             waits=(po[bk].wr() + [tcp, t_wg2]),
                             tick=True)
                    po[bk].set_w(tzz)
                t_a = act(zb1_v[:, 0:1024], pO[:, :], AF.Abs,
                          waits=[tzz] + R("zb1").wr())
                t_e = act(zb2_v[:, 0:1024], zb1_v[:, 0:1024], AF.Exp,
                          waits=[t_a] + R("zb2").wr(), scale=-1.0)
                t_l = act(zb1_v[:, 0:1024], zb2_v[:, 0:1024], AF.Ln,
                          waits=[t_e], bias=1.0)
                R("zb1").set_w(t_l)
                t_mn = kb.op("dv", lambda e: e.tensor_scalar_min(
                    zb2_v[:, 0:1024], pO[:, :], 0.0),
                    waits=[tzz, t_e] + (gkr.wr() if hf == 0 else []),
                    tick=True)
                R("zb2").set_w(t_mn)
                po[0].add_r(t_mn)
                po[1].add_r(t_mn)
                t_gk = kb.op("dv", lambda e, gb=gb, hf=hf: e.tensor_sub(
                    gkg_v[gb][:, hf * 1024:(hf + 1) * 1024],
                    zb2_v[:, 0:1024], zb1_v[:, 0:1024]),
                    waits=[t_l, t_mn], tick=True)
                R("zb1").add_r(t_gk)
                R("zb2").add_r(t_gk)
            gkr.set_w(t_gk)
            tw = kb.dma("gp", gk_d[tile * 4 + g, :, :], gkg_v[gb][:, :],
                        waits=[t_gk])
            gkr.add_r(tw)
            return t_gk

        vcr = [R("vc0"), R("vc1")]

        def emit_scan(tile, g, passb, t_kf, t_q, gk_ap, gk_reg,
                      Ktil_v, ATm_v, vsrc=None, vtok=None, split=False):
            glob_g = tile * NG + g
            vb = g % 2
            if vsrc is None:
                vC_v = vC_2[vb]
                tv = kb.dma("gp", vC_v[:, :], v_d[glob_g, :, :],
                            waits=vcr[vb].wr(), sem="vc%d" % vb)
                vcr[vb].set_w(tv)
            else:
                vC_v = vsrc
                tv = vtok
            t_gk = gk_reg.w

            for h in range(H):
                tg = mm(pBig[:, h * 128:(h + 1) * 128],
                        gk_ap[:, h * 128:(h + 1) * 128], u2_t[:, :],
                        True, True,
                        waits=(pbig_wr() + [t_gk, t_u2]) if h == 0 else (),
                        tick=(h == H - 1))
            for b in pb:
                b.set_w(tg)
            gl_view = pBig.ap().rearrange(
                "p (h l) -> p h l", h=H)[:, :, 127:128]
            t_egl = act(egl_f[:, 0:16].unsqueeze(2), gl_view, AF.Exp,
                        waits=[tg] + R("egl").wr(), scale=G16)
            R("egl").set_w(t_egl)
            for b in pb:
                b.add_r(t_egl)
            if passb:
                t_eqg = act(Eqg_v[:, :], pBig[:, :], AF.Exp,
                            waits=R("Eqg").wr(), scale=G16)
                R("Eqg").set_w(t_eqg)
                for b in pb:
                    b.add_r(t_eqg)
            else:
                t_gt = kb.op("dv", lambda e: e.tensor_add(
                    Gtot[:, :], Gtot[:, :], gl_view.squeeze(2)),
                    waits=[tg], tick=True)
                for b in pb:
                    b.add_r(t_gt)
            if passb:
                for h in range(H):
                    tg2 = mm(pBig[:, h * 128:(h + 1) * 128],
                             gk_ap[:, h * 128:(h + 1) * 128], u2x_t[:, :],
                             True, True,
                             waits=(pbig_wr() + [t_u2x]) if h == 0 else (),
                             tick=(h == H - 1))
                for b in pb:
                    b.set_w(tg2)
                t_eqh = act(EQh_v[:, :], pBig[:, :], AF.Exp,
                            waits=[tg2] + R("EQh").wr(), scale=G16)
                t_ekh = act(EKh_v[:, :], pBig[:, :], AF.Exp,
                            waits=R("EKh").wr(), scale=-G16)
                R("EQh").set_w(t_eqh)
                R("EKh").set_w(t_ekh)
                for b in pb:
                    b.add_r(t_ekh)
            for n in range(4):
                tr = mm(pBig[:, n * 512:(n + 1) * 512], v2_t[:, :],
                        gk_ap[:, n * 512:(n + 1) * 512], True, True,
                        waits=(pbig_wr() + [t_v2]) if n == 0 else (),
                        tick=(n == 3))
            for b in pb:
                b.set_w(tr)
            t_eks = act(EKs_v[:, :], pBig[:, :], AF.Exp,
                        waits=[tr] + R("EKs").wr(), scale=G16)
            R("EKs").set_w(t_eks)
            for b in pb:
                b.add_r(t_eks)
            gk_reg.add_r(t_eks)

            if passb:
                qsl = e3(q_v)[:, :, g * 128:(g + 1) * 128]
                ksl = e3(kF_v)[:, :, g * 128:(g + 1) * 128]
                t_qg = kb.op("dv", lambda e, qsl=qsl: e.tensor_tensor(
                    e3(qg_v), qsl, e3(Eqg_v), ALU.mult),
                    waits=[R("Eqg").w, t_q] + R("qg").wr(), tick=True)
                t_qh = kb.op("dv", lambda e, qsl=qsl: e.tensor_tensor(
                    e3(Qh_v), qsl, e3(EQh_v), ALU.mult),
                    waits=[t_eqh] + R("Qh").wr(), tick=True)
                t_kh = kb.op("dv", lambda e, ksl=ksl: e.tensor_tensor(
                    e3(Kh_v), ksl, e3(EKh_v), ALU.mult),
                    waits=[t_ekh, t_kf] + R("Kh").wr(), tick=True)
                R("qg").set_w(t_qg)
                R("Qh").set_w(t_qh)
                R("Kh").set_w(t_kh)
                R("Eqg").add_r(t_qg)
                R("EQh").add_r(t_qh)
                R("EKh").add_r(t_kh)

            for half in range(2):
                for hh in range(8):
                    h = half * 8 + hh
                    tt = kb.op("pe", lambda e, h=h, hh=hh: e.transpose(
                        pD[:, hh * 128:(hh + 1) * 128],
                        kF_v[:, h * 512 + g * 128: h * 512 + (g + 1) * 128],
                        id_t[:, :]),
                        waits=(pd_r.wr() + [t_kf, t_id]) if hh == 0 else (),
                        tick=(hh == 7))
                pd_r.set_w(tt)
                t_kt = kb.op("dv", lambda e, half=half: e.tensor_tensor(
                    Ktil_v[:, half * 1024:(half + 1) * 1024], pD[:, :],
                    EKs_v[:, half * 1024:(half + 1) * 1024], ALU.mult),
                    waits=[tt, t_eks] + (R("Ktil").wr() if half == 0 else []),
                    tick=True)
                pd_r.add_r(t_kt)
                R("Ktil").set_w(t_kt)
                R("EKs").add_r(t_kt)
            t_ktil = R("Ktil").w

            if passb:
                for r4 in range(4):
                    for hh in range(4):
                        h = r4 * 4 + hh
                        tA = mm(pC[:, hh * 128:(hh + 1) * 128],
                                Kh_v[:, h * 128:(h + 1) * 128],
                                Qh_v[:, h * 128:(h + 1) * 128], True, True,
                                waits=(pc_r.wr() + [R("Kh").w, R("Qh").w])
                                if hh == 0 else (), tick=(hh == 3))
                    pc_r.set_w(tA)
                    t_msk = kb.op("dv", lambda e, r4=r4: e.tensor_tensor(
                        ATm_v[:, r4 * 512:(r4 + 1) * 512], pC[:, :],
                        am8_t[:, :], ALU.mult),
                        waits=[tA, t_am] + (R("ATm").wr() if r4 == 0 else []),
                        tick=True)
                    pc_r.add_r(t_msk)
                    R("ATm").set_w(t_msk)
                t_atm = R("ATm").w
                for h in range(H):
                    mm(pBig[:, h * 128:(h + 1) * 128],
                       S_b[:, h * 128:(h + 1) * 128],
                       qg_v[:, h * 128:(h + 1) * 128], True, False,
                       waits=(pbig_wr() + [R("qg").w, R("S_b").w, t_atm, tv])
                       if h == 0 else ())
                    t_o = mm(pBig[:, h * 128:(h + 1) * 128],
                             vC_v[:, h * 128:(h + 1) * 128],
                             ATm_v[:, h * 128:(h + 1) * 128], False, True,
                             tick=(h == H - 1))
                for b in pb:
                    b.set_w(t_o)
                R("S_b").add_r(t_o)
                R("ATm").add_r(t_o)
                R("qg").add_r(t_o)
                if vsrc is None:
                    vcr[vb].add_r(t_o)
                t_sq = act(o2_v[:, :], pBig[:, :], AF.Square,
                           waits=[t_o] + R("o2").wr())
                R("o2").set_w(t_sq)
                for b in pb:
                    b.add_r(t_sq)
                for r4 in range(4):
                    t_cs = mm(pC[0:1, 0:512], onesb[:, :],
                              o2_v[:, r4 * 512:(r4 + 1) * 512], True, True,
                              waits=pc_r.wr() + [t_sq], tick=True)
                    pc_r.set_w(t_cs)
                    t_lnr = act(rows[0:1, 0:512], pC[0:1, 0:512], AF.Ln,
                                waits=[t_cs], scale=1.0 / 128.0, bias=1e-5)
                    pc_r.add_r(t_lnr)
                    t_rr = act(rows2[0:1, r4 * 512:(r4 + 1) * 512],
                               rows[0:1, 0:512], AF.Exp,
                               waits=[t_lnr] + (R("rows2").wr() if r4 == 0
                                                else []), scale=-0.5)
                    R("rows2").set_w(t_rr)
                t_rrow = R("rows2").w
                gview = e3(gs_v)[:, :, g * 128:(g + 1) * 128]
                t_og1 = kb.op("dv", lambda e, gview=gview: e.tensor_tensor(
                    e3(og1_v), e3(pBig.ap()), gview, ALU.mult),
                    waits=[t_o, t_sq, R("gs").w] + R("og1").wr(), tick=True)
                R("og1").set_w(t_og1)
                for b in pb:
                    b.add_r(t_og1)
                t_og2 = None
                for r4 in range(4):
                    bk = r4 % 2
                    t_rb = mm(pO[:, bk * 512:(bk + 1) * 512], nw_t[0:1, :],
                              rows2[0:1, r4 * 512:(r4 + 1) * 512], True, True,
                              waits=(po[bk].wr() + [t_rrow, t_nw]),
                              tick=True)
                    po[bk].set_w(t_rb)
                    t_og2 = kb.op("dv", lambda e, r4=r4, bk=bk:
                                  e.tensor_tensor(
                                      e3(og_v)[:, r4 * 4:(r4 + 1) * 4,
                                               g * 128:(g + 1) * 128],
                                      e3(og1_v)[:, r4 * 4:(r4 + 1) * 4, :],
                                      pO[:, bk * 512:(bk + 1) * 512]
                                      .rearrange("p (h l) -> p h l", h=4),
                                      ALU.mult),
                                  waits=[t_rb, t_og1] +
                                  (R("og").wr() if r4 == 0 else []),
                                  tick=True)
                    po[bk].add_r(t_og2)
                R("og").set_w(t_og2)
            if split:
                yield
            for h in range(H):
                t_kv = mm(pBig[:, h * 128:(h + 1) * 128],
                          Ktil_v[:, h * 128:(h + 1) * 128],
                          vC_v[:, h * 128:(h + 1) * 128], True, True,
                          waits=(pbig_wr() + [t_ktil, tv]) if h == 0
                          else (), tick=(h == H - 1))
            for b in pb:
                b.set_w(t_kv)
            R("Ktil").add_r(t_kv)
            if vsrc is None:
                vcr[vb].add_r(t_kv)
            else:
                R("vAll").add_r(t_kv)
            t_sdec = None
            for h in range(H):
                t_sdec = kb.op("dv", lambda e, h=h: e.tensor_scalar(
                    S_f[:, h * 128:(h + 1) * 128],
                    S_f[:, h * 128:(h + 1) * 128],
                    egl_f[:, h:h + 1], None, ALU.mult),
                    waits=(([R("egl").w] + R("S_f").wr()) if h == 0 else ()),
                    tick=(h == H - 1))
            t_sadd = kb.op("dv", lambda e: e.tensor_add(
                S_f[:, :], S_f[:, :], pBig.ap()), waits=[t_kv, t_sdec],
                tick=True)
            R("S_f").set_w(t_sadd)
            R("egl").add_r(t_sdec)
            for b in pb:
                b.add_r(t_sadd)
            if passb:
                t_sb = kb.op("dv", lambda e: e.tensor_copy(
                    S_b[:, :], S_f[:, :]), waits=R("S_b").wr() + [t_sadd],
                    tick=True)
                R("S_b").set_w(t_sb)

        x_reg, xn_reg = R("x"), R("xn")
        for tile in range(NT):
            pf = fence()
            tx = xpre
            if tile > 0:
                for c in range(KC):
                    tx = kb.dma("sy", x_t[:, c * 512:(c + 1) * 512],
                                x_d[c * 128:(c + 1) * 128,
                                    tile * 512:(tile + 1) * 512],
                                waits=(x_reg.wr() + pf) if c == 0 else ())
            x_reg.set_w(tx)
            dbg("xld_t%d" % tile, x_t, [tx], x_reg)
            t_xn, xtk, t_st = emit_ln(pf)
            dbg("xn_t%d" % tile, xn_t, [t_xn])
            if tile == 0:
                def mid_casts():
                    for wnm, srcd, dstd in (("k", wk_d, wkb_d),
                                            ("v", wv_d, wvb_d)):
                        CT[wnm] = kb.dma("gp", dstd[:, :, :], srcd[:, :, :],
                                         sem="cb" + wnm)
                t_f1 = emit_ffn(t_xn, pf, xtk, t_st, q="gp", gu=G_UP,
                                gg=G_GA, gd=G_DN, mid=mid_casts)
            else:
                def mid_cast_t(tile=tile):
                    wnm, srcd, dstd = (("q", wq_d, wqb_d),
                                       ("g", wgt_d, wgtb_d),
                                       ("o", wo_d, wob_d))[tile - 1]
                    CT[wnm] = kb.dma("gp", dstd[:, :, :], srcd[:, :, :],
                                     sem="cb" + wnm)
                t_f1 = emit_ffn(t_xn, pf, xtk, t_st, mid=mid_cast_t)
            dbg("x1_t%d" % tile, x_t, [t_f1], x_reg)
            t_xn1, xtk1, _ts1 = emit_ln([])
            dbg("xn1_t%d" % tile, xn_t, [t_xn1], xn_reg)
            def ev_k(m, psl, tmm, tile=tile):
                t = kb.op("dv", lambda e, m=m, psl=psl: e.tensor_copy(
                    kF_v[:, m * 512:(m + 1) * 512], psl),
                    waits=[tmm] + (R("kF").wr() if m == 0 else []), tick=True)
                R("kF").set_w(t)
                tw = kb.dma("gp", k_d[m, :, tile * 512:(tile + 1) * 512],
                            kF_v[:, m * 512:(m + 1) * 512], waits=[t])
                R("kF").add_r(tw)
                return t

            emit_proj(wkb_d, t_xn1, ev_k, fence() + [CT["k"]], xtk1)
            t_kf = R("kF").w
            for c in range(KC):
                twx = kb.dma("gp", x1_d[c, :, tile * 512:(tile + 1) * 512],
                             x_t[:, c * 512:(c + 1) * 512], waits=[t_f1])
                x_reg.add_r(twx)
                twn = kb.dma("gp", xn_d[c, :, tile * 512:(tile + 1) * 512],
                             xn_t[:, c * 512:(c + 1) * 512], waits=[t_xn1])
                xn_reg.add_r(twn)

            vr = R("vAll")
            for m in range(16):
                b = m % 4
                tl = kb.dma("sy", wab[b][:, :], wvb_d[m, :, :],
                            waits=war[b].wr() + (fence() + [CT["v"]]
                                                 if m < 4 else []),
                            sem="wa%d" % b)
                war[b].set_w(tl)
                bk = m % 2
                for t4 in range(4):
                    psl = pO[:, bk * 512 + t4 * 128:
                             bk * 512 + (t4 + 1) * 128]
                    for k in range(KC):
                        xw = [xtk1[k]]
                        tmm = mm(psl,
                                 xn_t[:, k * 512 + t4 * 128:
                                      k * 512 + (t4 + 1) * 128],
                                 wab[b][:, k * 128:(k + 1) * 128],
                                 k == 0, k == KC - 1,
                                 waits=(po[bk].wr() + [tl] + xw)
                                 if (t4 == 0 and k == 0) else xw,
                                 tick=(t4 == 3 and k == KC - 1))
                po[bk].set_w(tmm)
                war[b].add_r(tmm)
                tvv = kb.op("dv", lambda e, m=m, bk=bk: e.tensor_copy(
                    vAll_v.rearrange("p (a n) -> p a n", a=4)[
                        :, :, m * 128:(m + 1) * 128],
                    pO[:, bk * 512:(bk + 1) * 512].rearrange(
                        "p (a n) -> p a n", a=4)),
                    waits=[tmm] + (vr.wr() if m == 0 else []), tick=True)
                po[bk].add_r(tvv)
                vr.set_w(tvv)
            for t4 in range(4):
                tw = kb.dma("gp", v_d[tile * 4 + t4, :, :],
                            vAll_v[:, t4 * 2048:(t4 + 1) * 2048],
                            waits=[vr.w])
                vr.add_r(tw)

            dbg("vall0", vAll_v, [vr.w])
            emit_gk(tile, 0, 0, t_xn1, xtk1)
            for g in range(NG):
                gb = g % 2
                gen = emit_scan(tile, g, False, t_kf, None, gkg_v[gb],
                                R("gkg%d" % gb), KtilA_v, ATmA_v,
                                vsrc=vAll_v[:, g * 2048:(g + 1) * 2048],
                                vtok=vr.w, split=True)
                next(gen)
                if g + 1 < NG:
                    emit_gk(tile, g + 1, (g + 1) % 2, t_xn1, xtk1)
                for _ in gen:
                    pass

        def emit_combine():
            t_sS = kb.dma("gp", ccin_d[:, 0:2048], S_f[:, :],
                          waits=[R("S_f").w])
            t_sG = kb.dma("gp", ccin_d[:, 2048:2064], Gtot[:, :],
                          waits=[kb.tok("dv")])
            kb.cnt["cc"] += 1
            ccv = kb.cnt["cc"]
            for t in (t_sS, t_sG):
                kb._wait("gp", t)

            def run_cc(e):
                e.collective_compute(
                    "AllGather", ALU.bypass, replica_groups=[list(range(8))],
                    ins=[ccin_d.ap().opt()], outs=[ccout_d.ap().opt()],
                ).then_inc(kb.sems["cc"], 1)
            kb.steps["gp"].append(run_cc)
            t_cc = ("cc", ccv)

            pf = fence()
            t_gg = None
            for j in range(8):
                t_gg = kb.dma("gp", gG_v[:, j * 16:(j + 1) * 16],
                              ccout_d[j * 128:(j + 1) * 128, 2048:2064],
                              waits=[t_cc] + (pf if j == 0 else []))
            t_s0 = kb.op("dv", lambda e: e.memset(S0_v[:, :], 0.0),
                         waits=[t_cc] + pf, tick=True)
            t_acc = t_s0
            gsr = R("gS")
            for j in range(8):
                tSj = kb.dma("gp", gS_v[:, :],
                             ccout_d[j * 128:(j + 1) * 128, 0:2048],
                             waits=[t_cc] + gsr.wr())
                gsr.set_w(tSj)
                t_tj = kb.op("dv", lambda e, j=j: e.tensor_scalar(
                    tj_v[:, :], gG_v[:, 0:16], cm_t[:, j * 8: j * 8 + 1], None,
                    ALU.mult), waits=[t_gg, t_cm, t_s0], tick=True)
                for i in range(1, 8):
                    t_tj = kb.op("dv", lambda e, j=j, i=i: e.scalar_tensor_tensor(
                        tj_v[:, :], gG_v[:, i * 16:(i + 1) * 16],
                        cm_t[:, j * 8 + i: j * 8 + i + 1], tj_v[:, :],
                        ALU.mult, ALU.add), waits=[t_tj], tick=True)
                t_ej = act(ej_v[:, :], tj_v[:, :], AF.Exp, waits=[t_tj],
                           scale=G16)
                t_ejm = kb.op("dv", lambda e, j=j: e.tensor_scalar(
                    ej_v[:, :], ej_v[:, :], cm_t[:, 64 + j: 64 + j + 1], None,
                    ALU.mult), waits=[t_ej], tick=True)
                for h in range(H):
                    t_acc = kb.op("dv", lambda e, h=h: e.scalar_tensor_tensor(
                        S0_v[:, h * 128:(h + 1) * 128],
                        gS_v[:, h * 128:(h + 1) * 128],
                        ej_v[:, h:h + 1], S0_v[:, h * 128:(h + 1) * 128],
                        ALU.mult, ALU.add),
                        waits=[tSj, t_ejm, t_acc] if h == 0 else (),
                        tick=(h == H - 1))
                gsr.add_r(t_acc)
            t_scpS = kb.op("dv", lambda e: e.tensor_copy(S_f[:, :], S0_v[:, :]),
                           waits=[t_acc], tick=True)
            R("S_f").set_w(t_scpS)
            t_sbc = kb.op("dv", lambda e: e.tensor_copy(S_b[:, :], S_f[:, :]),
                          waits=R("S_b").wr() + [t_scpS], tick=True)
            R("S_b").set_w(t_sbc)
            dbg("S_A", Gtot[:, :], [kb.tok("dv")])
            dbg("S0", S_f[:, :], [R("S_f").w])


        for tile in range(NT):
            pf = fence()
            txn = None
            xn_ltoks = []
            for c in range(KC):
                txn = kb.dma("sy", xn_t[:, c * 512:(c + 1) * 512],
                             xn_d[c, :, tile * 512:(tile + 1) * 512],
                             waits=(xn_reg.wr() + pf) if c == 0 else ())
                xn_ltoks.append(txn)
            xn_reg.set_w(txn)
            t_kf = None
            for hM in range(H):
                t_kf = kb.dma("sy", kF_v[:, hM * 512:(hM + 1) * 512],
                              k_d[hM, :, tile * 512:(tile + 1) * 512],
                              waits=(R("kF").wr() + pf) if hM == 0 else ())
            R("kF").set_w(t_kf)

            def ev_q(m, psl, tmm):
                t = act(q_v[:, m * 512:(m + 1) * 512], psl, AF.Copy,
                        waits=[tmm] + (R("q").wr() if m == 0 else []),
                        scale=QSCALE)
                R("q").set_w(t)
                return t

            def ev_g(m, psl, tmm):
                t = act(gs_v[:, m * 512:(m + 1) * 512], psl, AF.Silu,
                        waits=[tmm] + (R("gs").wr() if m == 0 else []))
                R("gs").set_w(t)
                return t

            emit_proj(wqb_d, txn, ev_q, list(pf) + [CT["q"]])
            t_q = R("q").w
            emit_proj(wgtb_d, txn, ev_g, [CT["g"]])
            if tile == 0:
                emit_combine()
            for g in range(NG):
                gb = g % 2
                tgk = kb.dma("gp", gkg_v[gb][:, :], gk_d[tile * 4 + g, :, :],
                             waits=R("gkg%d" % gb).wr() +
                             (pf if g == 0 else []), sem="gkq%d" % gb)
                R("gkg%d" % gb).set_w(tgk)
                for _ in emit_scan(tile, g, True, t_kf, t_q, gkg_v[gb],
                                   R("gkg%d" % gb), KtilB_v, ATmB_v):
                    pass
            t_og = kb.tok("dv")
            dbg("og0", og_v, [t_og])
            tx = None
            for c in range(KC):
                tx = kb.dma("sy", x_t[:, c * 512:(c + 1) * 512],
                            x1_d[c, :, tile * 512:(tile + 1) * 512],
                            waits=(x_reg.wr() + [t_og]) if c == 0 else ())
            x_reg.set_w(tx)
            t_wo = None
            for m in range(16):
                b = m % 4
                tl = kb.dma("sy", wab[b][:, :], wob_d[m, :, :],
                            waits=war[b].wr() + [t_og, CT["o"]],
                            sem="wa%d" % b)
                war[b].set_w(tl)
                bk = m % 2
                psl = pO[:, bk * 512:(bk + 1) * 512]
                for k in range(KC):
                    tmm = mm(psl, wab[b][:, k * 128:(k + 1) * 128],
                             og_v[:, k * 512:(k + 1) * 512], k == 0,
                             k == KC - 1,
                             waits=(po[bk].wr() + [tl, t_og]) if k == 0
                             else (), tick=(k == KC - 1))
                po[bk].set_w(tmm)
                war[b].add_r(tmm)
                t_wo = kb.op("dv", lambda e, m=m, psl=psl: e.tensor_add(
                    x_t[:, m * 512:(m + 1) * 512],
                    x_t[:, m * 512:(m + 1) * 512], psl),
                    waits=[tmm, tx], tick=True)
                po[bk].add_r(t_wo)
                R("og").add_r(t_wo)
            x_reg.set_w(t_wo)
            dbg("xwo", x_t, [t_wo])
            t_xn2, xtk2, t_st2 = emit_ln(fence())
            t_f2 = emit_ffn(t_xn2, fence(), xtk2, t_st2)
            for c in range(KC):
                tw = kb.dma("sy", y_d[c * 128:(c + 1) * 128,
                                      tile * 512:(tile + 1) * 512],
                            x_t[:, c * 512:(c + 1) * 512], waits=[t_f2])
                x_reg.add_r(tw)

        kb._wait("sy", kb.tok("gp"))
        kb._wait("sy", kb.tok("sy"))

        @block.tensor
        def _(eng):
            for f in kb.steps["pe"]:
                f(eng)

        @block.vector
        def _(eng):
            for f in kb.steps["dv"]:
                f(eng)

        @block.scalar
        def _(eng):
            for f in kb.steps["ac"]:
                f(eng)

        @block.sync
        def _(eng):
            for f in kb.steps["sy"]:
                f(eng)

        @block.gpsimd
        def _(eng):
            for f in kb.steps["gp"]:
                f(eng)

    return nc


def kernel(**inputs):
    in_maps = host_prep(inputs)
    nc = build()
    res = run_bass_kernel_spmd(nc, in_maps, core_ids=list(range(8)))
    out = np.empty((2, 8192, D), np.float32)
    for c in range(8):
        b, s = c // 4, c % 4
        out[b, s * TSPAN:(s + 1) * TSPAN, :] = res.results[c]["y"].T
    return out


# version 22
# speedup vs baseline: 1.0004x; 1.0004x over previous
import numpy as np

import concourse.bass as bass
import concourse.mybir as mybir
from concourse.bass_utils import run_bass_kernel_spmd

F32 = mybir.dt.float32
BF16 = mybir.dt.bfloat16
FP8 = mybir.dt.float8e4
AF = mybir.ActivationFunctionType
ALU = mybir.AluOpType
AX = mybir.AxisListType

D = 2048
E = 8192
H = 16
CH = 64
TSPAN = 2048
P = 512
NT = TSPAN // P
KC = D // 128
NG = P // 128
ESLAB = 1024
NSLAB = E // ESLAB
QSCALE = 128.0 ** -0.5
DEBUG = False
G16 = 1.0 / 16.0


def _t(a):
    return np.ascontiguousarray(np.asarray(a, np.float32).T)


def _mpke(wt):
    return np.ascontiguousarray(
        wt.reshape(KC, 128, 16, 128).transpose(2, 1, 0, 3).reshape(16, 128, D))


def host_prep(inputs):
    x = np.asarray(inputs["L0_attn_x"], np.float32)
    wq = _mpke(np.ascontiguousarray(np.asarray(inputs["L0_attn_Wq"], np.float32)))
    wk = _mpke(np.ascontiguousarray(np.asarray(inputs["L0_attn_Wk"], np.float32)))
    wv = _mpke(np.ascontiguousarray(np.asarray(inputs["L0_attn_Wv"], np.float32)))
    wgt = _mpke(np.ascontiguousarray(np.asarray(inputs["L0_attn_Wgate"], np.float32)))
    wo = _mpke(np.ascontiguousarray(np.asarray(inputs["L0_attn_Wo"], np.float32)))
    wg1 = np.asarray(inputs["L0_attn_Wg1"], np.float32)
    wg1t = np.ascontiguousarray(
        wg1.reshape(KC, 128, 16).transpose(1, 0, 2).reshape(128, KC * 16))
    wg2a = np.ascontiguousarray(np.concatenate(
        [np.asarray(inputs["L0_attn_Wg2"], np.float32),
         np.asarray(inputs["L0_attn_bg2"], np.float32)[None, :]], axis=0))
    nw = np.ascontiguousarray(
        np.asarray(inputs["L0_attn_norm_w"], np.float32)[None, :])
    wupt = _t(inputs["L0_ffn_W_up"])
    wgat = _t(inputs["L0_ffn_W_gate"])
    wdnn = np.ascontiguousarray(np.asarray(inputs["L0_ffn_W_down"], np.float32))

    u2 = np.tril(np.ones((128, 128), np.float32)).T
    u2x = u2.copy()
    u2x[0, :] = 0.0
    v2 = (np.arange(128)[:, None] > np.arange(128)[None, :]).astype(np.float32)
    id128 = np.eye(128, dtype=np.float32)
    am = (np.arange(128)[:, None] <= np.arange(128)[None, :]).astype(np.float32)
    am8 = np.ascontiguousarray(np.tile(am, (1, 4)))

    in_maps = []
    for c in range(8):
        b, s = c // 4, c % 4
        xs = np.ascontiguousarray(x[b, s * TSPAN:(s + 1) * TSPAN, :].T)
        wcomb = np.zeros((8, 8), np.float32)
        mvec = np.zeros((8,), np.float32)
        for j in range(8):
            bj, sj = j // 4, j % 4
            if bj == b and sj < s:
                mvec[j] = 1.0
                for i in range(8):
                    bi, si = i // 4, i % 4
                    if bi == b and sj < si < s:
                        wcomb[j, i] = 1.0
        cmask = np.zeros((128, 72), np.float32)
        cmask[:, 0:64] = wcomb.reshape(1, 64)
        cmask[:, 64:72] = mvec.reshape(1, 8)
        wred = np.ascontiguousarray(np.stack([
            wupt[c * 256:c * 256 + 128], wupt[c * 256 + 128:c * 256 + 256],
            wgat[c * 256:c * 256 + 128], wgat[c * 256 + 128:c * 256 + 256],
            wdnn[c * 256:c * 256 + 128], wdnn[c * 256 + 128:c * 256 + 256],
        ]))
        in_maps.append({
            "x": xs, "wred": wred,
            "wq": wq, "wk": wk, "wv": wv, "wgt": wgt, "wo": wo,
            "wg1t": wg1t, "wg2a": wg2a, "nw": nw,
            "wupt": wupt, "wgat": wgat, "wdnn": wdnn,
            "u2": u2, "u2x": u2x, "v2": v2, "id128": id128, "am8": am8,
            "cmask": np.ascontiguousarray(cmask),
        })
    return in_maps


class Reg:
    __slots__ = ("w", "r")

    def __init__(self):
        self.w = None
        self.r = []

    def rd(self):
        return [self.w] if self.w else []

    def wr(self):
        out = list(self.r)
        if self.w:
            out.append(self.w)
        return out

    def set_w(self, tok):
        if tok is not None:
            self.w = tok
            self.r = []

    def add_r(self, tok):
        if tok is not None:
            self.r.append(tok)


class KB:
    ENGS = ("pe", "dv", "ac", "sy", "gp")

    def __init__(self):
        self.steps = {e: [] for e in self.ENGS}
        self.cnt = {e: 0 for e in ("pe", "dv", "ac", "sy", "gp", "cc")}
        self.sat = {e: {} for e in self.ENGS}
        self.sems = {}
        self.regs = {}

    def R(self, name):
        if name not in self.regs:
            self.regs[name] = Reg()
        return self.regs[name]

    def _wait(self, eng, tok):
        if tok is None:
            return
        sem, val = tok
        if val <= 0 or self.sat[eng].get(sem, 0) >= val:
            return
        self.sat[eng][sem] = val
        self.steps[eng].append(
            lambda e, s=sem, v=val: e.wait_ge(self.sems[s], v))

    def op(self, eng, fn, waits=(), tick=False):
        for t in waits:
            self._wait(eng, t)
        if tick:
            self.cnt[eng] += 1
            v = self.cnt[eng]

            def run(e, fn=fn, eng=eng):
                fn(e).then_inc(self.sems[eng], 1)
            self.steps[eng].append(run)
            return (eng, v)
        self.steps[eng].append(lambda e, fn=fn: fn(e))
        return None

    def dma(self, q, out_ap, in_ap, waits=(), sem=None, **kw):
        for t in waits:
            self._wait(q, t)
        s = sem or q
        self.cnt[s] = self.cnt.get(s, 0) + 16
        v = self.cnt[s]

        def run(e, o=out_ap, i=in_ap, s=s, kw=kw):
            e.dma_start(out=o, in_=i, **kw).then_inc(self.sems[s], 16)
        self.steps[q].append(run)
        return (s, v)

    def tok(self, eng):
        return (eng, self.cnt[eng])


def build():
    nc = bass.Bass("TRN2", target_bir_lowering=False)
    kb = KB()
    R = kb.R

    dt = nc.dram_tensor
    x_d = dt("x", [D, TSPAN], F32, kind="ExternalInput")
    y_d = dt("y", [D, TSPAN], F32, kind="ExternalOutput")
    wq_d = dt("wq", [16, 128, D], F32, kind="ExternalInput")
    wk_d = dt("wk", [16, 128, D], F32, kind="ExternalInput")
    wv_d = dt("wv", [16, 128, D], F32, kind="ExternalInput")
    wgt_d = dt("wgt", [16, 128, D], F32, kind="ExternalInput")
    wo_d = dt("wo", [16, 128, D], F32, kind="ExternalInput")
    wg1_d = dt("wg1t", [128, 256], F32, kind="ExternalInput")
    wg2_d = dt("wg2a", [17, D], F32, kind="ExternalInput")
    nw_d = dt("nw", [1, 128], F32, kind="ExternalInput")
    wupt_d = dt("wupt", [D, E], F32, kind="ExternalInput")
    wgat_d = dt("wgat", [D, E], F32, kind="ExternalInput")
    wdnn_d = dt("wdnn", [D, E], F32, kind="ExternalInput")
    u2_d = dt("u2", [128, 128], F32, kind="ExternalInput")
    u2x_d = dt("u2x", [128, 128], F32, kind="ExternalInput")
    v2_d = dt("v2", [128, 128], F32, kind="ExternalInput")
    id_d = dt("id128", [128, 128], F32, kind="ExternalInput")
    am8_d = dt("am8", [128, 512], F32, kind="ExternalInput")
    cm_d = dt("cmask", [128, 72], F32, kind="ExternalInput")
    wqb_d = dt("wqb", [16, 128, D], BF16)
    wkb_d = dt("wkb", [16, 128, D], BF16)
    wvb_d = dt("wvb", [16, 128, D], BF16)
    wgtb_d = dt("wgtb", [16, 128, D], BF16)
    wob_d = dt("wob", [16, 128, D], BF16)
    t8up_d = dt("t8up", [NSLAB, 128, KC * ESLAB], FP8)
    t8ga_d = dt("t8ga", [NSLAB, 128, KC * ESLAB], FP8)
    t8dn_d = dt("t8dn", [16, 128, E], FP8)
    x1_d = dt("x1s", [KC, 128, TSPAN], F32)
    xn_d = dt("xns", [KC, 128, TSPAN], BF16)
    k_d = dt("ks", [H, 128, TSPAN], BF16)
    v_d = dt("vs", [TSPAN // 128, 128, D], BF16)
    gk_d = dt("gks", [TSPAN // 128, 128, D], F32)
    ccin_d = dt("ccin", [128, 2064], F32)
    ccout_d = dt("ccout", [1024, 2064], F32)
    wred_d = dt("wred", [6, 128, 8192], F32, kind="ExternalInput")
    rin_d = dt("rin", [128, 4], F32)
    rout_d = dt("rout", [128, 4], F32)

    st = nc.sbuf_tensor
    from contextlib import ExitStack
    with ExitStack() as ctx:
        en = ctx.enter_context
        AR = en(st("AR", [128, 90112], BF16))
        S_f = en(st("S_f", [128, H * 128], F32))
        S_b = en(st("S_b", [128, H * 128], BF16))
        Gtot = en(st("Gtot", [128, H], F32))
        cm_t = en(st("cm_t", [128, 72], F32))
        u2_t = en(st("u2_t", [128, 128], F32))
        u2x_t = en(st("u2x_t", [128, 128], F32))
        v2_t = en(st("v2_t", [128, 128], F32))
        id_t = en(st("id_t", [128, 128], BF16))
        am8_t = en(st("am8_t", [128, 512], BF16))
        wg1_t = en(st("wg1_t", [128, 256], BF16))
        wg2_t = en(st("wg2_t", [17, D], BF16))
        nw_t = en(st("nw_t", [1, 128], BF16))
        onesb = en(st("onesb", [128, 1], BF16))
        onesr = en(st("onesr", [1, 128], F32))
        rows = en(st("rows", [1, 1536], F32))
        rows2 = en(st("rows2", [1, 2048], BF16))
        sc = en(st("sc", [128, 16], F32))
        accv = en(st("accv", [128, 4], F32))
        gred = en(st("gred", [128, 16], F32))
        egl_f = en(st("egl_f", [128, 32], F32))
        constt = en(st("constt", [128, 2], F32))
        pBig = en(nc.psum_tensor("pBig", [128, 2048], F32))
        pO = en(nc.psum_tensor("pO", [128, 1024], F32))
        pC = en(nc.psum_tensor("pC", [128, 512], F32))
        pD = en(nc.psum_tensor("pD", [128, 1024], BF16))
        s_pe = en(nc.semaphore("s_pe"))
        s_dv = en(nc.semaphore("s_dv"))
        s_ac = en(nc.semaphore("s_ac"))
        s_sy = en(nc.semaphore("s_sy"))
        s_gp = en(nc.semaphore("s_gp"))
        s_cc = en(nc.semaphore("s_cc"))
        extra_sems = {}
        for nm in ("wa0", "wa1", "wa2", "wa3", "wu0", "wu1", "wgx", "wd0",
                   "wd1", "vc0", "vc1", "gkq0", "gkq1", "ws0", "ws1",
                   "cbk", "cbv", "cbq", "cbg", "cbo", "t80", "t81",
                   "t84", "xp0", "xp1"):
            extra_sems[nm] = en(nc.semaphore("s_" + nm))
        block = en(nc.Block())
        kb.sems = {"pe": s_pe, "dv": s_dv, "ac": s_ac, "sy": s_sy,
                   "gp": s_gp, "cc": s_cc}
        kb.sems.update(extra_sems)

        ar_f = AR.ap().bitcast(F32)
        ar_b = AR.ap()
        ar_8 = AR.ap().bitcast(FP8)

        def fv(off, p, n):
            return ar_f[0:p, off // 4: off // 4 + n]

        def bv(off, p, n):
            return ar_b[0:p, off // 2: off // 2 + n]

        def v8(off, p, n):
            return ar_8[0:p, off: off + n]

        A0, A1, A2, A3, A4, A5 = (0, 32768, 49152, 114688, 147456, 163840)
        x_t = fv(A0, 128, 8192)
        q_v = bv(A0, 128, 8192)
        gs_v = bv(A0 + 16384, 128, 8192)
        xn_t = bv(A1, 128, 8192)
        ltb_v = fv(A1, 128, 4096)
        h_v = bv(A2, 128, 32768)
        wsl = [fv(A2, 128, 8192), fv(A2 + 32768, 128, 8192)]
        kF_v = bv(A2, 128, 8192)
        vAll_v = bv(A2 + 16384, 128, 8192)
        og_v = bv(A2 + 16384, 128, 8192)
        zb1_v = fv(A2 + 32768, 128, 2048)
        zb2_v = fv(A2 + 40960, 128, 2048)
        z1_v = bv(A2 + 49152, 32, 512)
        t1_v = fv(A2 + 50176, 128, 512)
        KtilA_v = bv(A2 + 52224, 128, 2048)
        ATmA_v = bv(A2 + 56320, 128, 2048)
        KtilB_v = bv(A2 + 32768, 128, 2048)
        ATmB_v = bv(A2 + 36864, 128, 2048)
        gS_v = fv(A2 + 40960, 128, 2048)
        gG_v = fv(A2 + 49152, 128, 128)
        tj_v = fv(A2 + 50176, 128, 16)
        ej_v = fv(A2 + 50432, 128, 16)
        S0_v = fv(A2 + 53248, 128, 2048)
        wupb = [v8(A3, 128, KC * ESLAB), v8(A3 + 16384, 128, KC * ESLAB)]
        wab = [bv(A3, 128, 2048), bv(A3 + 4096, 128, 2048),
               bv(A3 + 8192, 128, 2048), bv(A3 + 12288, 128, 2048)]
        Eqg_v = bv(A3 + 8192, 128, 2048)
        EQh_v = bv(A3 + 12288, 128, 2048)
        EKh_v = bv(A3 + 16384, 128, 2048)
        EKs_v = bv(A3 + 20480, 128, 2048)
        vC_2 = [bv(A3 + 24576, 128, 2048), bv(A3 + 28672, 128, 2048)]
        wgab = v8(A4, 128, KC * ESLAB)
        wdnb = [v8(A4, 128, E), v8(A4 + 8192, 128, E)]
        qg_v = bv(A4, 128, 2048)
        Qh_v = bv(A4 + 4096, 128, 2048)
        Kh_v = bv(A4 + 8192, 128, 2048)
        o2_v = bv(A4 + 12288, 128, 2048)
        og1_v = bv(A1, 128, 2048)
        xb_v = bv(A4, 128, 8192)
        silu_v = fv(A5, 128, 4096)
        gkg_v = [fv(A5, 128, 2048), fv(A5 + 8192, 128, 2048)]
        t8o_v2 = [v8(A5, 128, 8192), v8(A5 + 8192, 128, 8192)]
        xsq_v = bv(A5, 128, 8192)

        pb = [R("pb%d" % i) for i in range(4)]
        po = [R("po%d" % i) for i in range(2)]
        pc_r = R("pc")
        pd_r = R("pd")

        def pbig_wr():
            return [t for b in pb for t in b.wr()]

        def po_wr():
            return [t for b in po for t in b.wr()]

        def mm(out, lhsT, rhs, start, stop, waits=(), tick=False):
            return kb.op("pe", lambda e: e.matmul(out, lhsT, rhs, start=start,
                                                  stop=stop), waits, tick)

        def act(out, in_, func, waits=(), tick=True, **kw):
            return kb.op("ac", lambda e: e.activation(out, in_, func, **kw),
                         waits, tick)

        def fence():
            return [kb.tok(e) for e in ("pe", "dv", "ac", "sy", "gp")]

        e3 = lambda t: t.rearrange("p (h l) -> p h l", h=H)

        dbg_seen = set()

        def dbg(name, ap, waits, reg=None):
            if not DEBUG or name in dbg_seen:
                return
            dbg_seen.add(name)
            sh = list(ap.shape)
            dd = dt("dbg_" + name, sh, ap.dtype, kind="ExternalOutput")
            t = kb.dma("sy", dd.ap(), ap, waits=list(waits))
            if reg is not None:
                reg.add_r(t)

        t_cm = kb.dma("gp", cm_t[:, :], cm_d[:, :])
        t_u2 = kb.dma("gp", u2_t[:, :], u2_d[:, :])
        t_u2x = kb.dma("gp", u2x_t[:, :], u2x_d[:, :])
        t_v2 = kb.dma("gp", v2_t[:, :], v2_d[:, :])
        t_id = kb.dma("gp", id_t[:, :], id_d[:, :])
        t_am = kb.dma("gp", am8_t[:, :], am8_d[:, :])
        t_wg1 = kb.dma("gp", wg1_t[:, :], wg1_d[:, :])
        t_wg2 = kb.dma("gp", wg2_t[:, :], wg2_d[:, :])
        t_nw = kb.dma("gp", nw_t[:, :], nw_d[:, :])
        kb.op("dv", lambda e: e.memset(constt[:, 0:1], 1e-5))
        kb.op("dv", lambda e: e.memset(constt[:, 1:2], 1.0))
        nc.const_aps.register(F32, 1e-5, constt[:, 0:1])
        nc.const_aps.register(F32, 1.0, constt[:, 1:2])
        kb.op("dv", lambda e: e.memset(onesb[:, :], 1.0))
        kb.op("dv", lambda e: e.memset(onesr[:, :], 1.0))
        xpre = None
        for c in range(KC):
            xpre = kb.dma("gp", x_t[:, c * 512:(c + 1) * 512],
                          x_d[c * 128:(c + 1) * 128, 0:512], sem="xp0")
        kb.op("dv", lambda e: e.memset(S_f[:, :], 0.0))
        kb.op("dv", lambda e: e.memset(S_b[:, :], 0.0))
        kb.op("dv", lambda e: e.memset(Gtot[:, :], 0.0))
        t_mems = kb.op("dv", lambda e: e.memset(accv[:, :], 0.0), tick=True)

        wslr = [R("wsl0"), R("wsl1")]

        def reduce_mat(dram, nslab, width, col):
            tlast = None
            for i in range(nslab):
                b = i % 2
                tl = kb.dma("sy", wsl[b][:, 0:width],
                            dram[i * 128:(i + 1) * 128, :],
                            waits=wslr[b].wr(), sem="ws%d" % b)
                wslr[b].set_w(tl)
                t_red = kb.op("dv", lambda e, b=b, w=width: e.reduce_sum(
                    sc[:, 15:16], wsl[b][:, 0:w], axis=AX.X,
                    apply_absolute_value=True), waits=[tl, t_mems], tick=True)
                tlast = kb.op("dv", lambda e, c=col: e.tensor_add(
                    accv[:, c:c + 1], accv[:, c:c + 1], sc[:, 15:16]),
                    waits=[t_red], tick=True)
                wslr[b].add_r(tlast)
            return tlast

        tlast = None
        for i in range(6):
            b = i % 2
            tl = kb.dma("sy", wsl[b][:, :], wred_d[i, :, :],
                        waits=wslr[b].wr(), sem="ws%d" % b)
            wslr[b].set_w(tl)
            t_red = kb.op("dv", lambda e, b=b: e.reduce_sum(
                sc[:, 15:16], wsl[b][:, :], axis=AX.X,
                apply_absolute_value=True), waits=[tl, t_mems], tick=True)
            tlast = kb.op("dv", lambda e, c=i // 2: e.tensor_add(
                accv[:, c:c + 1], accv[:, c:c + 1], sc[:, 15:16]),
                waits=[t_red], tick=True)
            wslr[b].add_r(tlast)
        t_rw = kb.dma("gp", rin_d[:, :], accv[:, :], waits=[tlast])
        kb.cnt["cc"] += 1
        rcv = kb.cnt["cc"]
        kb._wait("gp", t_rw)

        def run_rcc(e):
            e.collective_compute(
                "AllReduce", ALU.add, replica_groups=[list(range(8))],
                ins=[rin_d.ap().opt()], outs=[rout_d.ap().opt()],
            ).then_inc(kb.sems["cc"], 1)
        kb.steps["gp"].append(run_rcc)
        t_rcc = ("cc", rcv)
        t_rrd = kb.dma("gp", accv[:, :], rout_d[:, :], waits=[t_rcc])
        CT = {}
        onescol = sc[:, 14:15]
        t_of = kb.op("dv", lambda e: e.memset(onescol, 1.0), tick=True)
        t_sum = mm(pC[0:1, 8:12], onescol, accv[:, 0:4], True, True,
                   waits=[t_of, t_rrd] + pc_r.wr(), tick=True)
        pc_r.set_w(t_sum)
        t_s = kb.op("dv", lambda e: e.tensor_scalar(
            rows[0:1, 0:4], pC[0:1, 8:12], float(2.0 ** -24), 1e-8,
            ALU.mult, ALU.add), waits=[t_sum], tick=True)
        pc_r.add_r(t_s)
        t_bc = mm(pC[:, 12:16], onesr[:, :], rows[0:1, 0:4], True, True,
                  waits=[t_s] + pc_r.wr(), tick=True)
        pc_r.set_w(t_bc)
        t_scp = kb.op("dv", lambda e: e.tensor_copy(sc[:, 0:4], pC[:, 12:16]),
                      waits=[t_bc], tick=True)
        pc_r.add_r(t_scp)
        kb.op("dv", lambda e: e.tensor_scalar_mul(sc[:, 4:7], sc[:, 0:3], 0.5),
              waits=[t_scp])
        kb.op("dv", lambda e: e.tensor_scalar_mul(sc[:, 7:10], sc[:, 0:3], -0.5))
        t_g1 = kb.op("dv", lambda e: e.tensor_tensor(
            sc[:, 10:11], sc[:, 1:2], sc[:, 2:3], ALU.mult), tick=True)
        t_scal = kb.op("dv", lambda e: e.tensor_scalar_mul(
            sc[:, 10:11], sc[:, 10:11], 0.5), waits=[t_g1], tick=True)
        dbg("sc", sc[:, :], [t_scal])

        t8r = [R("t8o0"), R("t8o1")]

        def quant_mat(dram, t8dram, nslab, width, tc, mc, dn):
            for i in range(nslab):
                b = i % 2
                t8b = i % 2
                t8o = t8o_v2[t8b]
                tl = kb.dma("sy", wsl[b][:, 0:width],
                            dram[i * 128:(i + 1) * 128, :],
                            waits=wslr[b].wr(), sem="ws%d" % b)
                wslr[b].set_w(tl)
                t2 = None
                for hf in range(max(1, width // 4096)):
                    w0 = hf * 4096
                    wd = min(4096, width)
                    t_lt = kb.op("dv", lambda e, b=b, w0=w0, wd=wd, mc=mc:
                                 e.tensor_scalar(ltb_v[:, 0:wd],
                                                 wsl[b][:, w0:w0 + wd],
                                                 sc[:, mc:mc + 1], None,
                                                 ALU.is_lt),
                                 waits=[tl, t_scal] +
                                 (t8r[t8b].wr() if hf == 0 else []),
                                 tick=True)
                    t2 = kb.op("dv", lambda e, t8o=t8o, b=b, w0=w0, wd=wd,
                               tc=tc: e.scalar_tensor_tensor(
                                   t8o[:, w0:w0 + wd],
                                   wsl[b][:, w0:w0 + wd],
                                   sc[:, tc:tc + 1], ltb_v[:, 0:wd],
                                   ALU.is_gt, ALU.subtract), waits=[t_lt],
                               tick=True)
                    wslr[b].add_r(t2)
                t8r[t8b].set_w(t2)
                if not dn:
                    tw = kb.dma("sy",
                                t8dram[:, :, i * ESLAB:(i + 1) * ESLAB]
                                .rearrange("s p e -> p s e"),
                                t8o[:, 0:8192].rearrange(
                                    "p (s e) -> p s e", s=NSLAB),
                                waits=[t2], sem="t8%d" % t8b)
                    t8r[t8b].add_r(tw)
                else:
                    tw = kb.dma("sy", t8dram[:, :, i * 128:(i + 1) * 128]
                                .rearrange("m p e -> p m e"),
                                t8o[:, 0:2048].rearrange(
                                    "p (m e) -> p m e", m=16), waits=[t2])
                    t8r[t8b].add_r(tw)

        quant_mat(wupt_d, t8up_d, 16, 8192, 4, 7, False)
        G_UP = [("t80", kb.cnt.get("t80", 0)), ("t81", kb.cnt.get("t81", 0))]
        quant_mat(wgat_d, t8ga_d, 16, 8192, 5, 8, False)
        G_GA = [("t80", kb.cnt.get("t80", 0)), ("t81", kb.cnt.get("t81", 0))]

        tb16_2 = [bv(A5, 128, 4096), bv(A5 + 8192, 128, 4096)]
        stg8 = v8(A3, 128, 8192)
        stg_r = R("stg8")
        tb_rs = [R("tb16a"), R("tb16b")]
        for i in range(16):
            b = i % 2
            tl = kb.dma("sy", wsl[b][:, 0:8192],
                        wdnn_d[i * 128:(i + 1) * 128, :],
                        waits=wslr[b].wr(), sem="ws%d" % b)
            wslr[b].set_w(tl)
            for hf in range(2):
                w0 = hf * 4096
                tb16 = tb16_2[hf]
                tb_r = tb_rs[hf]
                t_lt = kb.op("dv", lambda e, b=b, w0=w0:
                             e.tensor_scalar(ltb_v[:, 0:4096],
                                             wsl[b][:, w0:w0 + 4096],
                                             sc[:, 9:10], None, ALU.is_lt),
                             waits=[tl, t_scal] + tb_r.wr(), tick=True)
                t2 = kb.op("dv", lambda e, tb16=tb16, b=b, w0=w0:
                           e.scalar_tensor_tensor(
                               tb16[:, :], wsl[b][:, w0:w0 + 4096],
                               sc[:, 6:7], ltb_v[:, 0:4096],
                               ALU.is_gt, ALU.subtract), waits=[t_lt],
                           tick=True)
                wslr[b].add_r(t2)
                tb_r.set_w(t2)
                for kg in range(8):
                    bk = kg % 2
                    for kk in range(4):
                        k = kg * 4 + kk
                        tmm = mm(pO[:, bk * 512 + kk * 128:
                                    bk * 512 + (kk + 1) * 128],
                                 tb16[:, k * 128:(k + 1) * 128], id_t[:, :],
                                 True, True,
                                 waits=(po[bk].wr() + [t2, t_id])
                                 if kk == 0 else (), tick=(kk == 3))
                    po[bk].set_w(tmm)
                    tb_r.add_r(tmm)
                    t_ev = act(stg8[:, (hf * 32 + kg * 4) * 128:
                                    (hf * 32 + kg * 4 + 4) * 128],
                               pO[:, bk * 512:(bk + 1) * 512], AF.Copy,
                               waits=[tmm] + (stg_r.wr()
                                              if (hf == 0 and kg == 0)
                                              else []))
                    po[bk].add_r(t_ev)
                    stg_r.set_w(t_ev) if (hf == 1 and kg == 7) else                         stg_r.add_r(t_ev)
            tw = kb.dma("sy", t8dn_d[i, :, :], stg8[:, :], waits=[t_ev],
                        sem="t84")
            stg_r.add_r(tw)
        G_DN = [("t84", kb.cnt.get("t84", 0))]
        t_q8 = kb.tok("sy")

        def emit_ln(pf):
            xr, xnr = R("x"), R("xn")
            tc = None
            w0 = list(pf) + xr.rd() + R("xb").wr()
            for c in range(KC):
                tc = kb.op("dv", lambda e, c=c: e.tensor_copy(
                    xb_v[:, c * 512:(c + 1) * 512],
                    x_t[:, c * 512:(c + 1) * 512]),
                    waits=(w0 if c == 0 else ()), tick=(c == KC - 1))
            R("xb").set_w(tc)
            tsq = kb.op("dv", lambda e: e.tensor_tensor(
                xsq_v[:, :], xb_v[:, :], xb_v[:, :], ALU.mult),
                waits=R("xsq").wr(), tick=True)
            R("xsq").set_w(tsq)
            for c in range(KC):
                mm(pC[0:1, 0:512], onesb[:, :], xb_v[:, c * 512:(c + 1) * 512],
                   c == 0, c == KC - 1, waits=(pc_r.wr() + [tc]) if c == 0
                   else ())
            for c in range(KC):
                tS = mm(pO[0:1, 0:512], onesb[:, :],
                        xsq_v[:, c * 512:(c + 1) * 512], c == 0, c == KC - 1,
                        waits=(po_wr() + [tsq]) if c == 0 else (),
                        tick=(c == KC - 1))
            pc_r.set_w(tS)
            po[0].set_w(tS)
            t_m = kb.op("dv", lambda e: e.tensor_scalar_mul(
                rows[0:1, 0:512], pC[0:1, 0:512], 1.0 / 2048.0), waits=[tS],
                tick=True)
            t_rd = kb.op("dv", lambda e: e.tensor_scalar_mul(
                rows[0:1, 512:1024], pO[0:1, 0:512], 1.0 / 2048.0), tick=True)
            pc_r.add_r(t_rd)
            po[0].add_r(t_rd)
            t_m2 = kb.op("dv", lambda e: e.tensor_tensor(
                rows[0:1, 1024:1536], rows[0:1, 0:512], rows[0:1, 0:512],
                ALU.mult), waits=[t_m], tick=True)
            t_var = kb.op("dv", lambda e: e.tensor_sub(
                rows[0:1, 512:1024], rows[0:1, 512:1024],
                rows[0:1, 1024:1536]), waits=[t_rd, t_m2], tick=True)
            t_ln = act(rows[0:1, 1024:1536], rows[0:1, 512:1024], AF.Ln,
                       waits=[t_var], bias=1e-5)
            t_r = act(rows[0:1, 512:1024], rows[0:1, 1024:1536], AF.Exp,
                      waits=[t_ln], scale=-0.5)
            t_mr = kb.op("dv", lambda e: e.tensor_tensor(
                rows[0:1, 1024:1536], rows[0:1, 0:512], rows[0:1, 512:1024],
                ALU.mult), waits=[t_r, t_m], tick=True)
            mm(pO[:, 0:512], onesr[:, :], rows[0:1, 512:1024], True, True,
               waits=po_wr() + [t_r, t_mr])
            t_bc2 = mm(pO[:, 512:1024], onesr[:, :], rows[0:1, 1024:1536],
                       True, True, tick=True)
            po[0].set_w(t_bc2)
            po[1].set_w(t_bc2)
            t_xn = None
            xn_toks = []
            for c in range(KC):
                t_t1 = kb.op("dv", lambda e, c=c: e.tensor_tensor(
                    t1_v[:, :], x_t[:, c * 512:(c + 1) * 512], pO[:, 0:512],
                    ALU.mult),
                    waits=(([t_bc2] + xnr.wr() + R("t1").wr()) if c == 0
                           else ()), tick=True)
                t_xn = kb.op("dv", lambda e, c=c: e.tensor_sub(
                    xn_t[:, c * 512:(c + 1) * 512], t1_v[:, :],
                    pO[:, 512:1024]), waits=[t_t1], tick=True)
                xn_toks.append(t_xn)
            po[0].add_r(t_xn)
            po[1].add_r(t_xn)
            xnr.set_w(t_xn)
            R("t1").set_w(t_xn)
            return t_xn, xn_toks, t_rd

        def emit_ffn(t_xn, pf, xn_toks=None, t_stats=None, q="sy",
                     gu=None, gg=None, gd=None, mid=None):
            xr = R("x")
            wu = [R("wu0"), R("wu1")]
            wg = R("wg")
            hr, sr = R("h"), R("silu")
            t_h = None
            for s in range(NSLAB):
                b = s % 2
                tlu = kb.dma(q, wupb[b][:, :], t8up_d[s, :, :],
                             waits=wu[b].wr() +
                             ((gu if gu is not None else [t_q8]) + list(pf)
                              if s < 2 else []),
                             sem="wu%d" % b)
                wu[b].set_w(tlu)
                tlg = kb.dma(q, wgab[:, :], t8ga_d[s, :, :],
                             waits=wg.wr() +
                             ((gg if gg is not None else [t_q8]) +
                              [t_stats or t_xn] + list(pf)
                              if s == 0 else []),
                             sem="wgx")
                wg.set_w(tlg)
                silu_toks = []
                for m in range(8):
                    bk = m % 4
                    psl = pBig[:, bk * 512:(bk + 1) * 512]
                    for k in range(KC):
                        xw = [xn_toks[k]] if xn_toks else [t_xn]
                        tmm = mm(psl, wupb[b][:, k * ESLAB + m * 128:
                                              k * ESLAB + (m + 1) * 128],
                                 xn_t[:, k * 512:(k + 1) * 512],
                                 k == 0, k == KC - 1,
                                 waits=(pb[bk].wr() + [tlu] + xw)
                                 if k == 0 else xw, tick=(k == KC - 1))
                    pb[bk].set_w(tmm)
                    wu[b].add_r(tmm)
                    tsl = act(silu_v[:, m * 512:(m + 1) * 512], psl, AF.Silu,
                              waits=[tmm, t_scal] + (sr.wr() if m == 0
                                                     else []), scale=sc[:, 0:1])
                    pb[bk].add_r(tsl)
                    silu_toks.append(tsl)
                sr.set_w(silu_toks[-1])
                for m in range(8):
                    bk = m % 2
                    psl = pO[:, bk * 512:(bk + 1) * 512]
                    for k in range(KC):
                        xw = [xn_toks[k]] if xn_toks else [t_xn]
                        tmm = mm(psl, wgab[:, k * ESLAB + m * 128:
                                          k * ESLAB + (m + 1) * 128],
                                 xn_t[:, k * 512:(k + 1) * 512],
                                 k == 0, k == KC - 1,
                                 waits=(po[bk].wr() + [tlg] + xw)
                                 if k == 0 else xw, tick=(k == KC - 1))
                    po[bk].set_w(tmm)
                    wg.add_r(tmm)
                    t_h = kb.op("dv", lambda e, s=s, m=m, psl=psl:
                                e.tensor_tensor(
                                    h_v[:, (s * 8 + m) * 512:
                                        (s * 8 + m + 1) * 512],
                                    silu_v[:, m * 512:(m + 1) * 512], psl,
                                    ALU.mult),
                                waits=[tmm, silu_toks[m]] +
                                (hr.wr() if (s == 0 and m == 0) else []),
                                tick=True)
                    po[bk].add_r(t_h)
                    sr.add_r(t_h)
            hr.set_w(t_h)
            dbg("h_t%d" % kb.cnt.get("_ffn", 0), h_v, [t_h], hr)
            kb.cnt["_ffn"] = kb.cnt.get("_ffn", 0) + 1
            wd = [R("wd0"), R("wd1")]
            t_out = None
            t_gate_done = kb.tok("pe")
            if mid is not None:
                mid()
            for m in range(16):
                b = m % 2
                tld = kb.dma(q, wdnb[b][:, :], t8dn_d[m, :, :],
                             waits=wd[b].wr() +
                             ((gd if gd is not None else [t_q8]) +
                              [t_gate_done, t_h] if m < 2 else []),
                             sem="wd%d" % b)
                wd[b].set_w(tld)
                bk = m % 4
                psl = pBig[:, bk * 512:(bk + 1) * 512]
                for k in range(E // 128):
                    tmm = mm(psl, wdnb[b][:, k * 128:(k + 1) * 128],
                             h_v[:, k * 512:(k + 1) * 512],
                             k == 0, k == E // 128 - 1,
                             waits=(pb[bk].wr() + [tld, t_h]) if k == 0
                             else (), tick=(k == E // 128 - 1))
                pb[bk].set_w(tmm)
                wd[b].add_r(tmm)
                t_out = kb.op("dv", lambda e, m=m, psl=psl:
                              e.scalar_tensor_tensor(
                                  x_t[:, m * 512:(m + 1) * 512], psl,
                                  sc[:, 10:11], x_t[:, m * 512:(m + 1) * 512],
                                  ALU.mult, ALU.add),
                              waits=[tmm] + (xr.wr() if m == 0 else []),
                              tick=True)
                pb[bk].add_r(t_out)
                hr.add_r(t_out)
            xr.set_w(t_out)
            return t_out

        war = [R("wa0"), R("wa1"), R("wa2"), R("wa3")]

        def emit_proj(wdram, t_xn, evac, pf, xn_toks=None):
            t_ev = None
            for m in range(16):
                b = m % 4
                tl = kb.dma("sy", wab[b][:, :], wdram[m, :, :],
                            waits=war[b].wr() + (list(pf) if m < 4 else []),
                            sem="wa%d" % b)
                war[b].set_w(tl)
                bk = m % 2
                psl = pO[:, bk * 512:(bk + 1) * 512]
                for k in range(KC):
                    xw = [xn_toks[k]] if xn_toks else [t_xn]
                    tmm = mm(psl, wab[b][:, k * 128:(k + 1) * 128],
                             xn_t[:, k * 512:(k + 1) * 512], k == 0,
                             k == KC - 1,
                             waits=(po[bk].wr() + [tl] + xw) if k == 0
                             else xw, tick=(k == KC - 1))
                po[bk].set_w(tmm)
                war[b].add_r(tmm)
                t_ev = evac(m, psl, tmm)
                po[bk].add_r(t_ev)
            return t_ev

        def emit_gk(tile, g, gb, t_xn, xn_toks=None):
            gkr = R("gkg%d" % gb)
            for k in range(KC):
                xw = [xn_toks[k]] if xn_toks else [t_xn]
                tz = mm(pC[0:16, 0:512], wg1_t[:, k * 16:(k + 1) * 16],
                        xn_t[:, k * 512:(k + 1) * 512], k == 0, k == KC - 1,
                        waits=(pc_r.wr() + [t_wg1] + xw) if k == 0 else xw,
                        tick=(k == KC - 1))
            pc_r.set_w(tz)
            tms = kb.op("dv", lambda e: e.memset(z1_v[0:32, :], 1.0),
                        waits=R("z1").wr(), tick=True)
            tcp = kb.op("dv", lambda e: e.tensor_copy(
                z1_v[0:16, :], pC[0:16, 0:512]), waits=[tz, tms], tick=True)
            R("z1").set_w(tcp)
            pc_r.add_r(tcp)
            t_gk = None
            for hf in range(2):
                for n in range(2):
                    bk = n
                    tzz = mm(pO[:, bk * 512:(bk + 1) * 512],
                             z1_v[0:17, g * 128:(g + 1) * 128],
                             wg2_t[0:17, (hf * 2 + n) * 512:
                                   (hf * 2 + n + 1) * 512], True, True,
                             waits=(po[bk].wr() + [tcp, t_wg2]),
                             tick=True)
                    po[bk].set_w(tzz)
                t_a = act(zb1_v[:, 0:1024], pO[:, :], AF.Abs,
                          waits=[tzz] + R("zb1").wr())
                t_e = act(zb2_v[:, 0:1024], zb1_v[:, 0:1024], AF.Exp,
                          waits=[t_a] + R("zb2").wr(), scale=-1.0)
                t_l = act(zb1_v[:, 0:1024], zb2_v[:, 0:1024], AF.Ln,
                          waits=[t_e], bias=1.0)
                R("zb1").set_w(t_l)
                t_mn = kb.op("dv", lambda e: e.tensor_scalar_min(
                    zb2_v[:, 0:1024], pO[:, :], 0.0),
                    waits=[tzz, t_e] + (gkr.wr() if hf == 0 else []),
                    tick=True)
                R("zb2").set_w(t_mn)
                po[0].add_r(t_mn)
                po[1].add_r(t_mn)
                t_gk = kb.op("dv", lambda e, gb=gb, hf=hf: e.tensor_sub(
                    gkg_v[gb][:, hf * 1024:(hf + 1) * 1024],
                    zb2_v[:, 0:1024], zb1_v[:, 0:1024]),
                    waits=[t_l, t_mn], tick=True)
                R("zb1").add_r(t_gk)
                R("zb2").add_r(t_gk)
            gkr.set_w(t_gk)
            tw = kb.dma("gp", gk_d[tile * 4 + g, :, :], gkg_v[gb][:, :],
                        waits=[t_gk])
            gkr.add_r(tw)
            return t_gk

        vcr = [R("vc0"), R("vc1")]

        def emit_scan(tile, g, passb, t_kf, t_q, gk_ap, gk_reg,
                      Ktil_v, ATm_v, vsrc=None, vtok=None, split=False):
            glob_g = tile * NG + g
            vb = g % 2
            if vsrc is None:
                vC_v = vC_2[vb]
                tv = kb.dma("gp", vC_v[:, :], v_d[glob_g, :, :],
                            waits=vcr[vb].wr(), sem="vc%d" % vb)
                vcr[vb].set_w(tv)
            else:
                vC_v = vsrc
                tv = vtok
            t_gk = gk_reg.w

            for h in range(H):
                tg = mm(pBig[:, h * 128:(h + 1) * 128],
                        gk_ap[:, h * 128:(h + 1) * 128], u2_t[:, :],
                        True, True,
                        waits=(pbig_wr() + [t_gk, t_u2]) if h == 0 else (),
                        tick=(h == H - 1))
            for b in pb:
                b.set_w(tg)
            gl_view = pBig.ap().rearrange(
                "p (h l) -> p h l", h=H)[:, :, 127:128]
            t_egl = act(egl_f[:, 0:16].unsqueeze(2), gl_view, AF.Exp,
                        waits=[tg] + R("egl").wr(), scale=G16)
            R("egl").set_w(t_egl)
            for b in pb:
                b.add_r(t_egl)
            if passb:
                t_eqg = act(Eqg_v[:, :], pBig[:, :], AF.Exp,
                            waits=R("Eqg").wr(), scale=G16)
                R("Eqg").set_w(t_eqg)
                for b in pb:
                    b.add_r(t_eqg)
            else:
                t_gt = kb.op("dv", lambda e: e.tensor_add(
                    Gtot[:, :], Gtot[:, :], gl_view.squeeze(2)),
                    waits=[tg], tick=True)
                for b in pb:
                    b.add_r(t_gt)
            if passb:
                for h in range(H):
                    tg2 = mm(pBig[:, h * 128:(h + 1) * 128],
                             gk_ap[:, h * 128:(h + 1) * 128], u2x_t[:, :],
                             True, True,
                             waits=(pbig_wr() + [t_u2x]) if h == 0 else (),
                             tick=(h == H - 1))
                for b in pb:
                    b.set_w(tg2)
                t_eqh = act(EQh_v[:, :], pBig[:, :], AF.Exp,
                            waits=[tg2] + R("EQh").wr(), scale=G16)
                t_ekh = act(EKh_v[:, :], pBig[:, :], AF.Exp,
                            waits=R("EKh").wr(), scale=-G16)
                R("EQh").set_w(t_eqh)
                R("EKh").set_w(t_ekh)
                for b in pb:
                    b.add_r(t_ekh)
            for n in range(4):
                tr = mm(pBig[:, n * 512:(n + 1) * 512], v2_t[:, :],
                        gk_ap[:, n * 512:(n + 1) * 512], True, True,
                        waits=(pbig_wr() + [t_v2]) if n == 0 else (),
                        tick=(n == 3))
            for b in pb:
                b.set_w(tr)
            t_eks = act(EKs_v[:, :], pBig[:, :], AF.Exp,
                        waits=[tr] + R("EKs").wr(), scale=G16)
            R("EKs").set_w(t_eks)
            for b in pb:
                b.add_r(t_eks)
            gk_reg.add_r(t_eks)

            if passb:
                qsl = e3(q_v)[:, :, g * 128:(g + 1) * 128]
                ksl = e3(kF_v)[:, :, g * 128:(g + 1) * 128]
                t_qg = kb.op("dv", lambda e, qsl=qsl: e.tensor_tensor(
                    e3(qg_v), qsl, e3(Eqg_v), ALU.mult),
                    waits=[R("Eqg").w, t_q] + R("qg").wr(), tick=True)
                t_qh = kb.op("dv", lambda e, qsl=qsl: e.tensor_tensor(
                    e3(Qh_v), qsl, e3(EQh_v), ALU.mult),
                    waits=[t_eqh] + R("Qh").wr(), tick=True)
                t_kh = kb.op("dv", lambda e, ksl=ksl: e.tensor_tensor(
                    e3(Kh_v), ksl, e3(EKh_v), ALU.mult),
                    waits=[t_ekh, t_kf] + R("Kh").wr(), tick=True)
                R("qg").set_w(t_qg)
                R("Qh").set_w(t_qh)
                R("Kh").set_w(t_kh)
                R("Eqg").add_r(t_qg)
                R("EQh").add_r(t_qh)
                R("EKh").add_r(t_kh)

            for half in range(2):
                for hh in range(8):
                    h = half * 8 + hh
                    tt = kb.op("pe", lambda e, h=h, hh=hh: e.transpose(
                        pD[:, hh * 128:(hh + 1) * 128],
                        kF_v[:, h * 512 + g * 128: h * 512 + (g + 1) * 128],
                        id_t[:, :]),
                        waits=(pd_r.wr() + [t_kf, t_id]) if hh == 0 else (),
                        tick=(hh == 7))
                pd_r.set_w(tt)
                t_kt = kb.op("dv", lambda e, half=half: e.tensor_tensor(
                    Ktil_v[:, half * 1024:(half + 1) * 1024], pD[:, :],
                    EKs_v[:, half * 1024:(half + 1) * 1024], ALU.mult),
                    waits=[tt, t_eks] + (R("Ktil").wr() if half == 0 else []),
                    tick=True)
                pd_r.add_r(t_kt)
                R("Ktil").set_w(t_kt)
                R("EKs").add_r(t_kt)
            t_ktil = R("Ktil").w

            if passb:
                for r4 in range(4):
                    for hh in range(4):
                        h = r4 * 4 + hh
                        tA = mm(pC[:, hh * 128:(hh + 1) * 128],
                                Kh_v[:, h * 128:(h + 1) * 128],
                                Qh_v[:, h * 128:(h + 1) * 128], True, True,
                                waits=(pc_r.wr() + [R("Kh").w, R("Qh").w])
                                if hh == 0 else (), tick=(hh == 3))
                    pc_r.set_w(tA)
                    t_msk = kb.op("dv", lambda e, r4=r4: e.tensor_tensor(
                        ATm_v[:, r4 * 512:(r4 + 1) * 512], pC[:, :],
                        am8_t[:, :], ALU.mult),
                        waits=[tA, t_am] + (R("ATm").wr() if r4 == 0 else []),
                        tick=True)
                    pc_r.add_r(t_msk)
                    R("ATm").set_w(t_msk)
                t_atm = R("ATm").w
                for h in range(H):
                    mm(pBig[:, h * 128:(h + 1) * 128],
                       S_b[:, h * 128:(h + 1) * 128],
                       qg_v[:, h * 128:(h + 1) * 128], True, False,
                       waits=(pbig_wr() + [R("qg").w, R("S_b").w, t_atm, tv])
                       if h == 0 else ())
                    t_o = mm(pBig[:, h * 128:(h + 1) * 128],
                             vC_v[:, h * 128:(h + 1) * 128],
                             ATm_v[:, h * 128:(h + 1) * 128], False, True,
                             tick=(h == H - 1))
                for b in pb:
                    b.set_w(t_o)
                R("S_b").add_r(t_o)
                R("ATm").add_r(t_o)
                R("qg").add_r(t_o)
                if vsrc is None:
                    vcr[vb].add_r(t_o)
                t_sq = act(o2_v[:, :], pBig[:, :], AF.Square,
                           waits=[t_o] + R("o2").wr())
                R("o2").set_w(t_sq)
                for b in pb:
                    b.add_r(t_sq)
                for r4 in range(4):
                    t_cs = mm(pC[0:1, 0:512], onesb[:, :],
                              o2_v[:, r4 * 512:(r4 + 1) * 512], True, True,
                              waits=pc_r.wr() + [t_sq], tick=True)
                    pc_r.set_w(t_cs)
                    t_lnr = act(rows[0:1, 0:512], pC[0:1, 0:512], AF.Ln,
                                waits=[t_cs], scale=1.0 / 128.0, bias=1e-5)
                    pc_r.add_r(t_lnr)
                    t_rr = act(rows2[0:1, r4 * 512:(r4 + 1) * 512],
                               rows[0:1, 0:512], AF.Exp,
                               waits=[t_lnr] + (R("rows2").wr() if r4 == 0
                                                else []), scale=-0.5)
                    R("rows2").set_w(t_rr)
                t_rrow = R("rows2").w
                gview = e3(gs_v)[:, :, g * 128:(g + 1) * 128]
                t_og1 = kb.op("dv", lambda e, gview=gview: e.tensor_tensor(
                    e3(og1_v), e3(pBig.ap()), gview, ALU.mult),
                    waits=[t_o, t_sq, R("gs").w] + R("og1").wr(), tick=True)
                R("og1").set_w(t_og1)
                for b in pb:
                    b.add_r(t_og1)
                t_og2 = None
                for r4 in range(4):
                    bk = r4 % 2
                    t_rb = mm(pO[:, bk * 512:(bk + 1) * 512], nw_t[0:1, :],
                              rows2[0:1, r4 * 512:(r4 + 1) * 512], True, True,
                              waits=(po[bk].wr() + [t_rrow, t_nw]),
                              tick=True)
                    po[bk].set_w(t_rb)
                    t_og2 = kb.op("dv", lambda e, r4=r4, bk=bk:
                                  e.tensor_tensor(
                                      e3(og_v)[:, r4 * 4:(r4 + 1) * 4,
                                               g * 128:(g + 1) * 128],
                                      e3(og1_v)[:, r4 * 4:(r4 + 1) * 4, :],
                                      pO[:, bk * 512:(bk + 1) * 512]
                                      .rearrange("p (h l) -> p h l", h=4),
                                      ALU.mult),
                                  waits=[t_rb, t_og1] +
                                  (R("og").wr() if r4 == 0 else []),
                                  tick=True)
                    po[bk].add_r(t_og2)
                R("og").set_w(t_og2)
            if split:
                yield
            for h in range(H):
                t_kv = mm(pBig[:, h * 128:(h + 1) * 128],
                          Ktil_v[:, h * 128:(h + 1) * 128],
                          vC_v[:, h * 128:(h + 1) * 128], True, True,
                          waits=(pbig_wr() + [t_ktil, tv]) if h == 0
                          else (), tick=(h == H - 1))
            for b in pb:
                b.set_w(t_kv)
            R("Ktil").add_r(t_kv)
            if vsrc is None:
                vcr[vb].add_r(t_kv)
            else:
                R("vAll").add_r(t_kv)
            t_sdec = None
            for h in range(H):
                t_sdec = kb.op("dv", lambda e, h=h: e.tensor_scalar(
                    S_f[:, h * 128:(h + 1) * 128],
                    S_f[:, h * 128:(h + 1) * 128],
                    egl_f[:, h:h + 1], None, ALU.mult),
                    waits=(([R("egl").w] + R("S_f").wr()) if h == 0 else ()),
                    tick=(h == H - 1))
            t_sadd = kb.op("dv", lambda e: e.tensor_add(
                S_f[:, :], S_f[:, :], pBig.ap()), waits=[t_kv, t_sdec],
                tick=True)
            R("S_f").set_w(t_sadd)
            R("egl").add_r(t_sdec)
            for b in pb:
                b.add_r(t_sadd)
            if passb:
                t_sb = kb.op("dv", lambda e: e.tensor_copy(
                    S_b[:, :], S_f[:, :]), waits=R("S_b").wr() + [t_sadd],
                    tick=True)
                R("S_b").set_w(t_sb)

        x_reg, xn_reg = R("x"), R("xn")
        for tile in range(NT):
            pf = fence()
            if tile >= 1:
                wnm, srcd, dstd = (("q", wq_d, wqb_d), ("g", wgt_d, wgtb_d),
                                   ("o", wo_d, wob_d))[tile - 1]
                CT[wnm] = kb.dma("gp", dstd[:, :, :], srcd[:, :, :],
                                 sem="cb" + wnm)
            tx = xpre
            if tile > 0:
                for c in range(KC):
                    tx = kb.dma("sy", x_t[:, c * 512:(c + 1) * 512],
                                x_d[c * 128:(c + 1) * 128,
                                    tile * 512:(tile + 1) * 512],
                                waits=(x_reg.wr() + pf) if c == 0 else ())
            x_reg.set_w(tx)
            dbg("xld_t%d" % tile, x_t, [tx], x_reg)
            t_xn, xtk, t_st = emit_ln(pf)
            dbg("xn_t%d" % tile, xn_t, [t_xn])
            if tile == 0:
                def mid_casts():
                    for wnm, srcd, dstd in (("k", wk_d, wkb_d),
                                            ("v", wv_d, wvb_d)):
                        CT[wnm] = kb.dma("gp", dstd[:, :, :], srcd[:, :, :],
                                         sem="cb" + wnm)
                t_f1 = emit_ffn(t_xn, pf, xtk, t_st, q="gp", gu=G_UP,
                                gg=G_GA, gd=G_DN, mid=mid_casts)
            else:
                t_f1 = emit_ffn(t_xn, pf, xtk, t_st)
            dbg("x1_t%d" % tile, x_t, [t_f1], x_reg)
            t_xn1, xtk1, _ts1 = emit_ln([])
            dbg("xn1_t%d" % tile, xn_t, [t_xn1], xn_reg)
            def ev_k(m, psl, tmm, tile=tile):
                t = kb.op("dv", lambda e, m=m, psl=psl: e.tensor_copy(
                    kF_v[:, m * 512:(m + 1) * 512], psl),
                    waits=[tmm] + (R("kF").wr() if m == 0 else []), tick=True)
                R("kF").set_w(t)
                tw = kb.dma("gp", k_d[m, :, tile * 512:(tile + 1) * 512],
                            kF_v[:, m * 512:(m + 1) * 512], waits=[t])
                R("kF").add_r(tw)
                return t

            emit_proj(wkb_d, t_xn1, ev_k, fence() + [CT["k"]], xtk1)
            t_kf = R("kF").w
            for c in range(KC):
                twx = kb.dma("gp", x1_d[c, :, tile * 512:(tile + 1) * 512],
                             x_t[:, c * 512:(c + 1) * 512], waits=[t_f1])
                x_reg.add_r(twx)
                twn = kb.dma("gp", xn_d[c, :, tile * 512:(tile + 1) * 512],
                             xn_t[:, c * 512:(c + 1) * 512], waits=[t_xn1])
                xn_reg.add_r(twn)

            vr = R("vAll")
            for m in range(16):
                b = m % 4
                tl = kb.dma("sy", wab[b][:, :], wvb_d[m, :, :],
                            waits=war[b].wr() + (fence() + [CT["v"]]
                                                 if m < 4 else []),
                            sem="wa%d" % b)
                war[b].set_w(tl)
                bk = m % 2
                for t4 in range(4):
                    psl = pO[:, bk * 512 + t4 * 128:
                             bk * 512 + (t4 + 1) * 128]
                    for k in range(KC):
                        xw = [xtk1[k]]
                        tmm = mm(psl,
                                 xn_t[:, k * 512 + t4 * 128:
                                      k * 512 + (t4 + 1) * 128],
                                 wab[b][:, k * 128:(k + 1) * 128],
                                 k == 0, k == KC - 1,
                                 waits=(po[bk].wr() + [tl] + xw)
                                 if (t4 == 0 and k == 0) else xw,
                                 tick=(t4 == 3 and k == KC - 1))
                po[bk].set_w(tmm)
                war[b].add_r(tmm)
                tvv = kb.op("dv", lambda e, m=m, bk=bk: e.tensor_copy(
                    vAll_v.rearrange("p (a n) -> p a n", a=4)[
                        :, :, m * 128:(m + 1) * 128],
                    pO[:, bk * 512:(bk + 1) * 512].rearrange(
                        "p (a n) -> p a n", a=4)),
                    waits=[tmm] + (vr.wr() if m == 0 else []), tick=True)
                po[bk].add_r(tvv)
                vr.set_w(tvv)
            for t4 in range(4):
                tw = kb.dma("gp", v_d[tile * 4 + t4, :, :],
                            vAll_v[:, t4 * 2048:(t4 + 1) * 2048],
                            waits=[vr.w])
                vr.add_r(tw)

            dbg("vall0", vAll_v, [vr.w])
            emit_gk(tile, 0, 0, t_xn1, xtk1)
            for g in range(NG):
                gb = g % 2
                gen = emit_scan(tile, g, False, t_kf, None, gkg_v[gb],
                                R("gkg%d" % gb), KtilA_v, ATmA_v,
                                vsrc=vAll_v[:, g * 2048:(g + 1) * 2048],
                                vtok=vr.w, split=True)
                next(gen)
                if g + 1 < NG:
                    emit_gk(tile, g + 1, (g + 1) % 2, t_xn1, xtk1)
                for _ in gen:
                    pass

        def emit_combine():
            t_sS = kb.dma("gp", ccin_d[:, 0:2048], S_f[:, :],
                          waits=[R("S_f").w])
            t_sG = kb.dma("gp", ccin_d[:, 2048:2064], Gtot[:, :],
                          waits=[kb.tok("dv")])
            kb.cnt["cc"] += 1
            ccv = kb.cnt["cc"]
            for t in (t_sS, t_sG):
                kb._wait("gp", t)

            def run_cc(e):
                e.collective_compute(
                    "AllGather", ALU.bypass, replica_groups=[list(range(8))],
                    ins=[ccin_d.ap().opt()], outs=[ccout_d.ap().opt()],
                ).then_inc(kb.sems["cc"], 1)
            kb.steps["gp"].append(run_cc)
            t_cc = ("cc", ccv)

            pf = fence()
            t_gg = None
            for j in range(8):
                t_gg = kb.dma("gp", gG_v[:, j * 16:(j + 1) * 16],
                              ccout_d[j * 128:(j + 1) * 128, 2048:2064],
                              waits=[t_cc] + (pf if j == 0 else []))
            t_s0 = kb.op("dv", lambda e: e.memset(S0_v[:, :], 0.0),
                         waits=[t_cc] + pf, tick=True)
            t_acc = t_s0
            gsr = R("gS")
            for j in range(8):
                tSj = kb.dma("gp", gS_v[:, :],
                             ccout_d[j * 128:(j + 1) * 128, 0:2048],
                             waits=[t_cc] + gsr.wr())
                gsr.set_w(tSj)
                t_tj = kb.op("dv", lambda e, j=j: e.tensor_scalar(
                    tj_v[:, :], gG_v[:, 0:16], cm_t[:, j * 8: j * 8 + 1], None,
                    ALU.mult), waits=[t_gg, t_cm, t_s0], tick=True)
                for i in range(1, 8):
                    t_tj = kb.op("dv", lambda e, j=j, i=i: e.scalar_tensor_tensor(
                        tj_v[:, :], gG_v[:, i * 16:(i + 1) * 16],
                        cm_t[:, j * 8 + i: j * 8 + i + 1], tj_v[:, :],
                        ALU.mult, ALU.add), waits=[t_tj], tick=True)
                t_ej = act(ej_v[:, :], tj_v[:, :], AF.Exp, waits=[t_tj],
                           scale=G16)
                t_ejm = kb.op("dv", lambda e, j=j: e.tensor_scalar(
                    ej_v[:, :], ej_v[:, :], cm_t[:, 64 + j: 64 + j + 1], None,
                    ALU.mult), waits=[t_ej], tick=True)
                for h in range(H):
                    t_acc = kb.op("dv", lambda e, h=h: e.scalar_tensor_tensor(
                        S0_v[:, h * 128:(h + 1) * 128],
                        gS_v[:, h * 128:(h + 1) * 128],
                        ej_v[:, h:h + 1], S0_v[:, h * 128:(h + 1) * 128],
                        ALU.mult, ALU.add),
                        waits=[tSj, t_ejm, t_acc] if h == 0 else (),
                        tick=(h == H - 1))
                gsr.add_r(t_acc)
            t_scpS = kb.op("dv", lambda e: e.tensor_copy(S_f[:, :], S0_v[:, :]),
                           waits=[t_acc], tick=True)
            R("S_f").set_w(t_scpS)
            t_sbc = kb.op("dv", lambda e: e.tensor_copy(S_b[:, :], S_f[:, :]),
                          waits=R("S_b").wr() + [t_scpS], tick=True)
            R("S_b").set_w(t_sbc)
            dbg("S_A", Gtot[:, :], [kb.tok("dv")])
            dbg("S0", S_f[:, :], [R("S_f").w])


        for tile in range(NT):
            pf = fence()
            qxb = "sy" if tile == 0 else "gp"
            sxb = None if tile == 0 else ("xp%d" % (tile % 2))
            txn = None
            xn_ltoks = []
            for c in range(KC):
                txn = kb.dma(qxb, xn_t[:, c * 512:(c + 1) * 512],
                             xn_d[c, :, tile * 512:(tile + 1) * 512],
                             waits=(xn_reg.wr() + pf) if c == 0 else (),
                             sem=sxb)
                xn_ltoks.append(txn)
            xn_reg.set_w(txn)
            t_kf = None
            for hM in range(H):
                t_kf = kb.dma(qxb, kF_v[:, hM * 512:(hM + 1) * 512],
                              k_d[hM, :, tile * 512:(tile + 1) * 512],
                              waits=(R("kF").wr() + pf) if hM == 0 else (),
                              sem=sxb)
            R("kF").set_w(t_kf)

            def ev_q(m, psl, tmm):
                t = act(q_v[:, m * 512:(m + 1) * 512], psl, AF.Copy,
                        waits=[tmm] + (R("q").wr() if m == 0 else []),
                        scale=QSCALE)
                R("q").set_w(t)
                return t

            def ev_g(m, psl, tmm):
                t = act(gs_v[:, m * 512:(m + 1) * 512], psl, AF.Silu,
                        waits=[tmm] + (R("gs").wr() if m == 0 else []))
                R("gs").set_w(t)
                return t

            emit_proj(wqb_d, txn, ev_q, list(pf) + [CT["q"]])
            t_q = R("q").w
            emit_proj(wgtb_d, txn, ev_g, [CT["g"]])
            if tile == 0:
                emit_combine()
            for g in range(NG):
                gb = g % 2
                tgk = kb.dma("gp", gkg_v[gb][:, :], gk_d[tile * 4 + g, :, :],
                             waits=R("gkg%d" % gb).wr() +
                             (pf if g == 0 else []), sem="gkq%d" % gb)
                R("gkg%d" % gb).set_w(tgk)
                for _ in emit_scan(tile, g, True, t_kf, t_q, gkg_v[gb],
                                   R("gkg%d" % gb), KtilB_v, ATmB_v):
                    pass
            t_og = kb.tok("dv")
            dbg("og0", og_v, [t_og])
            tx = None
            for c in range(KC):
                tx = kb.dma("sy", x_t[:, c * 512:(c + 1) * 512],
                            x1_d[c, :, tile * 512:(tile + 1) * 512],
                            waits=(x_reg.wr() + [t_og]) if c == 0 else ())
            x_reg.set_w(tx)
            t_wo = None
            for m in range(16):
                b = m % 4
                tl = kb.dma("sy", wab[b][:, :], wob_d[m, :, :],
                            waits=war[b].wr() + [t_og, CT["o"]],
                            sem="wa%d" % b)
                war[b].set_w(tl)
                bk = m % 2
                psl = pO[:, bk * 512:(bk + 1) * 512]
                for k in range(KC):
                    tmm = mm(psl, wab[b][:, k * 128:(k + 1) * 128],
                             og_v[:, k * 512:(k + 1) * 512], k == 0,
                             k == KC - 1,
                             waits=(po[bk].wr() + [tl, t_og]) if k == 0
                             else (), tick=(k == KC - 1))
                po[bk].set_w(tmm)
                war[b].add_r(tmm)
                t_wo = kb.op("dv", lambda e, m=m, psl=psl: e.tensor_add(
                    x_t[:, m * 512:(m + 1) * 512],
                    x_t[:, m * 512:(m + 1) * 512], psl),
                    waits=[tmm, tx], tick=True)
                po[bk].add_r(t_wo)
                R("og").add_r(t_wo)
            x_reg.set_w(t_wo)
            dbg("xwo", x_t, [t_wo])
            t_xn2, xtk2, t_st2 = emit_ln(fence())
            t_f2 = emit_ffn(t_xn2, fence(), xtk2, t_st2)
            for c in range(KC):
                tw = kb.dma("sy", y_d[c * 128:(c + 1) * 128,
                                      tile * 512:(tile + 1) * 512],
                            x_t[:, c * 512:(c + 1) * 512], waits=[t_f2])
                x_reg.add_r(tw)

        kb._wait("sy", kb.tok("gp"))
        kb._wait("sy", kb.tok("sy"))

        @block.tensor
        def _(eng):
            for f in kb.steps["pe"]:
                f(eng)

        @block.vector
        def _(eng):
            for f in kb.steps["dv"]:
                f(eng)

        @block.scalar
        def _(eng):
            for f in kb.steps["ac"]:
                f(eng)

        @block.sync
        def _(eng):
            for f in kb.steps["sy"]:
                f(eng)

        @block.gpsimd
        def _(eng):
            for f in kb.steps["gp"]:
                f(eng)

    return nc


def kernel(**inputs):
    in_maps = host_prep(inputs)
    nc = build()
    res = run_bass_kernel_spmd(nc, in_maps, core_ids=list(range(8)))
    out = np.empty((2, 8192, D), np.float32)
    for c in range(8):
        b, s = c // 4, c % 4
        out[b, s * TSPAN:(s + 1) * TSPAN, :] = res.results[c]["y"].T
    return out


# version 23
# speedup vs baseline: 1.0060x; 1.0055x over previous
import numpy as np

import concourse.bass as bass
import concourse.mybir as mybir
from concourse.bass_utils import run_bass_kernel_spmd

F32 = mybir.dt.float32
BF16 = mybir.dt.bfloat16
FP8 = mybir.dt.float8e4
AF = mybir.ActivationFunctionType
ALU = mybir.AluOpType
AX = mybir.AxisListType

D = 2048
E = 8192
H = 16
CH = 64
TSPAN = 2048
P = 512
NT = TSPAN // P
KC = D // 128
NG = P // 128
ESLAB = 1024
NSLAB = E // ESLAB
QSCALE = 128.0 ** -0.5
DEBUG = False
G16 = 1.0 / 16.0


def _t(a):
    return np.ascontiguousarray(np.asarray(a, np.float32).T)


def _mpke(wt):
    return np.ascontiguousarray(
        wt.reshape(KC, 128, 16, 128).transpose(2, 1, 0, 3).reshape(16, 128, D))


def host_prep(inputs):
    x = np.asarray(inputs["L0_attn_x"], np.float32)
    wq = _mpke(np.ascontiguousarray(np.asarray(inputs["L0_attn_Wq"], np.float32)))
    wk = _mpke(np.ascontiguousarray(np.asarray(inputs["L0_attn_Wk"], np.float32)))
    wv = _mpke(np.ascontiguousarray(np.asarray(inputs["L0_attn_Wv"], np.float32)))
    wgt = _mpke(np.ascontiguousarray(np.asarray(inputs["L0_attn_Wgate"], np.float32)))
    wo = _mpke(np.ascontiguousarray(np.asarray(inputs["L0_attn_Wo"], np.float32)))
    wg1 = np.asarray(inputs["L0_attn_Wg1"], np.float32)
    wg1t = np.ascontiguousarray(
        wg1.reshape(KC, 128, 16).transpose(1, 0, 2).reshape(128, KC * 16))
    wg2a = np.ascontiguousarray(np.concatenate(
        [np.asarray(inputs["L0_attn_Wg2"], np.float32),
         np.asarray(inputs["L0_attn_bg2"], np.float32)[None, :]], axis=0))
    nw = np.ascontiguousarray(
        np.asarray(inputs["L0_attn_norm_w"], np.float32)[None, :])
    wupt = _t(inputs["L0_ffn_W_up"])
    wgat = _t(inputs["L0_ffn_W_gate"])
    wdnn = np.ascontiguousarray(np.asarray(inputs["L0_ffn_W_down"], np.float32))

    u2 = np.tril(np.ones((128, 128), np.float32)).T
    u2x = u2.copy()
    u2x[0, :] = 0.0
    v2 = (np.arange(128)[:, None] > np.arange(128)[None, :]).astype(np.float32)
    id128 = np.eye(128, dtype=np.float32)
    am = (np.arange(128)[:, None] <= np.arange(128)[None, :]).astype(np.float32)
    am8 = np.ascontiguousarray(np.tile(am, (1, 4)))

    in_maps = []
    for c in range(8):
        b, s = c // 4, c % 4
        xs = np.ascontiguousarray(x[b, s * TSPAN:(s + 1) * TSPAN, :].T)
        wcomb = np.zeros((8, 8), np.float32)
        mvec = np.zeros((8,), np.float32)
        for j in range(8):
            bj, sj = j // 4, j % 4
            if bj == b and sj < s:
                mvec[j] = 1.0
                for i in range(8):
                    bi, si = i // 4, i % 4
                    if bi == b and sj < si < s:
                        wcomb[j, i] = 1.0
        cmask = np.zeros((128, 72), np.float32)
        cmask[:, 0:64] = wcomb.reshape(1, 64)
        cmask[:, 64:72] = mvec.reshape(1, 8)
        wred = np.ascontiguousarray(np.stack([
            wupt[c * 256:c * 256 + 128], wupt[c * 256 + 128:c * 256 + 256],
            wgat[c * 256:c * 256 + 128], wgat[c * 256 + 128:c * 256 + 256],
            wdnn[c * 256:c * 256 + 128], wdnn[c * 256 + 128:c * 256 + 256],
        ]))
        in_maps.append({
            "x": xs, "wred": wred,
            "wq": wq, "wk": wk, "wv": wv, "wgt": wgt, "wo": wo,
            "wg1t": wg1t, "wg2a": wg2a, "nw": nw,
            "wupt": wupt, "wgat": wgat, "wdnn": wdnn,
            "u2": u2, "u2x": u2x, "v2": v2, "id128": id128, "am8": am8,
            "cmask": np.ascontiguousarray(cmask),
        })
    return in_maps


class Reg:
    __slots__ = ("w", "r")

    def __init__(self):
        self.w = None
        self.r = []

    def rd(self):
        return [self.w] if self.w else []

    def wr(self):
        out = list(self.r)
        if self.w:
            out.append(self.w)
        return out

    def set_w(self, tok):
        if tok is not None:
            self.w = tok
            self.r = []

    def add_r(self, tok):
        if tok is not None:
            self.r.append(tok)


class KB:
    ENGS = ("pe", "dv", "ac", "sy", "gp")

    def __init__(self):
        self.steps = {e: [] for e in self.ENGS}
        self.cnt = {e: 0 for e in ("pe", "dv", "ac", "sy", "gp", "cc")}
        self.sat = {e: {} for e in self.ENGS}
        self.sems = {}
        self.regs = {}

    def R(self, name):
        if name not in self.regs:
            self.regs[name] = Reg()
        return self.regs[name]

    def _wait(self, eng, tok):
        if tok is None:
            return
        sem, val = tok
        if val <= 0 or self.sat[eng].get(sem, 0) >= val:
            return
        self.sat[eng][sem] = val
        self.steps[eng].append(
            lambda e, s=sem, v=val: e.wait_ge(self.sems[s], v))

    def op(self, eng, fn, waits=(), tick=False):
        for t in waits:
            self._wait(eng, t)
        if tick:
            self.cnt[eng] += 1
            v = self.cnt[eng]

            def run(e, fn=fn, eng=eng):
                fn(e).then_inc(self.sems[eng], 1)
            self.steps[eng].append(run)
            return (eng, v)
        self.steps[eng].append(lambda e, fn=fn: fn(e))
        return None

    def dma(self, q, out_ap, in_ap, waits=(), sem=None, **kw):
        for t in waits:
            self._wait(q, t)
        s = sem or q
        self.cnt[s] = self.cnt.get(s, 0) + 16
        v = self.cnt[s]

        def run(e, o=out_ap, i=in_ap, s=s, kw=kw):
            e.dma_start(out=o, in_=i, **kw).then_inc(self.sems[s], 16)
        self.steps[q].append(run)
        return (s, v)

    def tok(self, eng):
        return (eng, self.cnt[eng])


def build():
    nc = bass.Bass("TRN2", target_bir_lowering=False)
    kb = KB()
    R = kb.R

    dt = nc.dram_tensor
    x_d = dt("x", [D, TSPAN], F32, kind="ExternalInput")
    y_d = dt("y", [D, TSPAN], F32, kind="ExternalOutput")
    wq_d = dt("wq", [16, 128, D], F32, kind="ExternalInput")
    wk_d = dt("wk", [16, 128, D], F32, kind="ExternalInput")
    wv_d = dt("wv", [16, 128, D], F32, kind="ExternalInput")
    wgt_d = dt("wgt", [16, 128, D], F32, kind="ExternalInput")
    wo_d = dt("wo", [16, 128, D], F32, kind="ExternalInput")
    wg1_d = dt("wg1t", [128, 256], F32, kind="ExternalInput")
    wg2_d = dt("wg2a", [17, D], F32, kind="ExternalInput")
    nw_d = dt("nw", [1, 128], F32, kind="ExternalInput")
    wupt_d = dt("wupt", [D, E], F32, kind="ExternalInput")
    wgat_d = dt("wgat", [D, E], F32, kind="ExternalInput")
    wdnn_d = dt("wdnn", [D, E], F32, kind="ExternalInput")
    u2_d = dt("u2", [128, 128], F32, kind="ExternalInput")
    u2x_d = dt("u2x", [128, 128], F32, kind="ExternalInput")
    v2_d = dt("v2", [128, 128], F32, kind="ExternalInput")
    id_d = dt("id128", [128, 128], F32, kind="ExternalInput")
    am8_d = dt("am8", [128, 512], F32, kind="ExternalInput")
    cm_d = dt("cmask", [128, 72], F32, kind="ExternalInput")
    wqb_d = dt("wqb", [16, 128, D], BF16)
    wkb_d = dt("wkb", [16, 128, D], BF16)
    wvb_d = dt("wvb", [16, 128, D], BF16)
    wgtb_d = dt("wgtb", [16, 128, D], BF16)
    wob_d = dt("wob", [16, 128, D], BF16)
    t8up_d = dt("t8up", [NSLAB, 128, KC * ESLAB], FP8)
    t8ga_d = dt("t8ga", [NSLAB, 128, KC * ESLAB], FP8)
    t8dn_d = dt("t8dn", [16, 128, E], FP8)
    x1_d = dt("x1s", [KC, 128, TSPAN], F32)
    xn_d = dt("xns", [KC, 128, TSPAN], BF16)
    k_d = dt("ks", [H, 128, TSPAN], BF16)
    v_d = dt("vs", [TSPAN // 128, 128, D], BF16)
    gk_d = dt("gks", [TSPAN // 128, 128, D], F32)
    ccin_d = dt("ccin", [128, 2064], F32)
    ccout_d = dt("ccout", [1024, 2064], F32)
    wred_d = dt("wred", [6, 128, 8192], F32, kind="ExternalInput")
    rin_d = dt("rin", [128, 4], F32)
    rout_d = dt("rout", [128, 4], F32)

    st = nc.sbuf_tensor
    from contextlib import ExitStack
    with ExitStack() as ctx:
        en = ctx.enter_context
        AR = en(st("AR", [128, 90112], BF16))
        S_f = en(st("S_f", [128, H * 128], F32))
        S_b = en(st("S_b", [128, H * 128], BF16))
        Gtot = en(st("Gtot", [128, H], F32))
        cm_t = en(st("cm_t", [128, 72], F32))
        u2_t = en(st("u2_t", [128, 128], F32))
        u2x_t = en(st("u2x_t", [128, 128], F32))
        v2_t = en(st("v2_t", [128, 128], F32))
        id_t = en(st("id_t", [128, 128], BF16))
        am8_t = en(st("am8_t", [128, 512], BF16))
        wg1_t = en(st("wg1_t", [128, 256], BF16))
        wg2_t = en(st("wg2_t", [17, D], BF16))
        nw_t = en(st("nw_t", [1, 128], BF16))
        onesb = en(st("onesb", [128, 1], BF16))
        onesr = en(st("onesr", [1, 128], F32))
        rows = en(st("rows", [1, 1536], F32))
        rows2 = en(st("rows2", [1, 2048], BF16))
        sc = en(st("sc", [128, 16], F32))
        accv = en(st("accv", [128, 4], F32))
        gred = en(st("gred", [128, 16], F32))
        egl_f = en(st("egl_f", [128, 32], F32))
        constt = en(st("constt", [128, 2], F32))
        pBig = en(nc.psum_tensor("pBig", [128, 2048], F32))
        pO = en(nc.psum_tensor("pO", [128, 1024], F32))
        pC = en(nc.psum_tensor("pC", [128, 512], F32))
        pD = en(nc.psum_tensor("pD", [128, 1024], BF16))
        s_pe = en(nc.semaphore("s_pe"))
        s_dv = en(nc.semaphore("s_dv"))
        s_ac = en(nc.semaphore("s_ac"))
        s_sy = en(nc.semaphore("s_sy"))
        s_gp = en(nc.semaphore("s_gp"))
        s_cc = en(nc.semaphore("s_cc"))
        extra_sems = {}
        for nm in ("wa0", "wa1", "wa2", "wa3", "wu0", "wu1", "wgx", "wd0",
                   "wd1", "vc0", "vc1", "gkq0", "gkq1", "ws0", "ws1",
                   "cbk", "cbv", "cbq", "cbg", "cbo", "t80", "t81",
                   "t84", "xp0"):
            extra_sems[nm] = en(nc.semaphore("s_" + nm))
        block = en(nc.Block())
        kb.sems = {"pe": s_pe, "dv": s_dv, "ac": s_ac, "sy": s_sy,
                   "gp": s_gp, "cc": s_cc}
        kb.sems.update(extra_sems)

        ar_f = AR.ap().bitcast(F32)
        ar_b = AR.ap()
        ar_8 = AR.ap().bitcast(FP8)

        def fv(off, p, n):
            return ar_f[0:p, off // 4: off // 4 + n]

        def bv(off, p, n):
            return ar_b[0:p, off // 2: off // 2 + n]

        def v8(off, p, n):
            return ar_8[0:p, off: off + n]

        A0, A1, A2, A3, A4, A5 = (0, 32768, 49152, 114688, 147456, 163840)
        x_t = fv(A0, 128, 8192)
        q_v = bv(A0, 128, 8192)
        gs_v = bv(A0 + 16384, 128, 8192)
        xn_t = bv(A1, 128, 8192)
        ltb_v = fv(A1, 128, 4096)
        h_v = bv(A2, 128, 32768)
        wsl = [fv(A2, 128, 8192), fv(A2 + 32768, 128, 8192)]
        kF_v = bv(A2, 128, 8192)
        vAll_v = bv(A2 + 16384, 128, 8192)
        og_v = bv(A2 + 16384, 128, 8192)
        zb1_v = fv(A2 + 32768, 128, 2048)
        zb2_v = fv(A2 + 40960, 128, 2048)
        z1_v = bv(A2 + 49152, 32, 512)
        t1_v = fv(A2 + 50176, 128, 512)
        KtilA_v = bv(A2 + 52224, 128, 2048)
        ATmA_v = bv(A2 + 56320, 128, 2048)
        KtilB_v = bv(A2 + 32768, 128, 2048)
        ATmB_v = bv(A2 + 36864, 128, 2048)
        gS_v = fv(A2 + 40960, 128, 2048)
        gG_v = fv(A2 + 49152, 128, 128)
        tj_v = fv(A2 + 50176, 128, 16)
        ej_v = fv(A2 + 50432, 128, 16)
        S0_v = fv(A2 + 53248, 128, 2048)
        wupb = [v8(A3, 128, KC * ESLAB), v8(A3 + 16384, 128, KC * ESLAB)]
        wab = [bv(A3, 128, 2048), bv(A3 + 4096, 128, 2048),
               bv(A3 + 8192, 128, 2048), bv(A3 + 12288, 128, 2048)]
        Eqg_v = bv(A3 + 8192, 128, 2048)
        EQh_v = bv(A3 + 12288, 128, 2048)
        EKh_v = bv(A3 + 16384, 128, 2048)
        EKs_v = bv(A3 + 20480, 128, 2048)
        vC_2 = [bv(A3 + 24576, 128, 2048), bv(A3 + 28672, 128, 2048)]
        wgab = v8(A4, 128, KC * ESLAB)
        wdnb = [v8(A4, 128, E), v8(A4 + 8192, 128, E)]
        qg_v = bv(A4, 128, 2048)
        Qh_v = bv(A4 + 4096, 128, 2048)
        Kh_v = bv(A4 + 8192, 128, 2048)
        o2_v = bv(A4 + 12288, 128, 2048)
        og1_v = bv(A1, 128, 2048)
        xb_v = bv(A4, 128, 8192)
        silu_v = fv(A5, 128, 4096)
        gkg_v = [fv(A5, 128, 2048), fv(A5 + 8192, 128, 2048)]
        t8o_v2 = [v8(A5, 128, 8192), v8(A5 + 8192, 128, 8192)]
        xsq_v = bv(A5, 128, 8192)

        pb = [R("pb%d" % i) for i in range(4)]
        po = [R("po%d" % i) for i in range(2)]
        pc_r = R("pc")
        pd_r = R("pd")

        def pbig_wr():
            return [t for b in pb for t in b.wr()]

        def po_wr():
            return [t for b in po for t in b.wr()]

        def mm(out, lhsT, rhs, start, stop, waits=(), tick=False):
            return kb.op("pe", lambda e: e.matmul(out, lhsT, rhs, start=start,
                                                  stop=stop), waits, tick)

        def act(out, in_, func, waits=(), tick=True, **kw):
            return kb.op("ac", lambda e: e.activation(out, in_, func, **kw),
                         waits, tick)

        def fence():
            return [kb.tok(e) for e in ("pe", "dv", "ac", "sy", "gp")]

        e3 = lambda t: t.rearrange("p (h l) -> p h l", h=H)

        dbg_seen = set()

        def dbg(name, ap, waits, reg=None):
            if not DEBUG or name in dbg_seen:
                return
            dbg_seen.add(name)
            sh = list(ap.shape)
            dd = dt("dbg_" + name, sh, ap.dtype, kind="ExternalOutput")
            t = kb.dma("sy", dd.ap(), ap, waits=list(waits))
            if reg is not None:
                reg.add_r(t)

        t_cm = kb.dma("gp", cm_t[:, :], cm_d[:, :])
        t_u2 = kb.dma("gp", u2_t[:, :], u2_d[:, :])
        t_u2x = kb.dma("gp", u2x_t[:, :], u2x_d[:, :])
        t_v2 = kb.dma("gp", v2_t[:, :], v2_d[:, :])
        t_id = kb.dma("gp", id_t[:, :], id_d[:, :])
        t_am = kb.dma("gp", am8_t[:, :], am8_d[:, :])
        t_wg1 = kb.dma("gp", wg1_t[:, :], wg1_d[:, :])
        t_wg2 = kb.dma("gp", wg2_t[:, :], wg2_d[:, :])
        t_nw = kb.dma("gp", nw_t[:, :], nw_d[:, :])
        kb.op("dv", lambda e: e.memset(constt[:, 0:1], 1e-5))
        kb.op("dv", lambda e: e.memset(constt[:, 1:2], 1.0))
        nc.const_aps.register(F32, 1e-5, constt[:, 0:1])
        nc.const_aps.register(F32, 1.0, constt[:, 1:2])
        kb.op("dv", lambda e: e.memset(onesb[:, :], 1.0))
        kb.op("dv", lambda e: e.memset(onesr[:, :], 1.0))
        xpre = None
        for c in range(KC):
            xpre = kb.dma("gp", x_t[:, c * 512:(c + 1) * 512],
                          x_d[c * 128:(c + 1) * 128, 0:512], sem="xp0")
        kb.op("dv", lambda e: e.memset(S_f[:, :], 0.0))
        kb.op("dv", lambda e: e.memset(S_b[:, :], 0.0))
        kb.op("dv", lambda e: e.memset(Gtot[:, :], 0.0))
        t_mems = kb.op("dv", lambda e: e.memset(accv[:, :], 0.0), tick=True)

        wslr = [R("wsl0"), R("wsl1")]

        def reduce_mat(dram, nslab, width, col):
            tlast = None
            for i in range(nslab):
                b = i % 2
                tl = kb.dma("sy", wsl[b][:, 0:width],
                            dram[i * 128:(i + 1) * 128, :],
                            waits=wslr[b].wr(), sem="ws%d" % b)
                wslr[b].set_w(tl)
                t_red = kb.op("dv", lambda e, b=b, w=width: e.reduce_sum(
                    sc[:, 15:16], wsl[b][:, 0:w], axis=AX.X,
                    apply_absolute_value=True), waits=[tl, t_mems], tick=True)
                tlast = kb.op("dv", lambda e, c=col: e.tensor_add(
                    accv[:, c:c + 1], accv[:, c:c + 1], sc[:, 15:16]),
                    waits=[t_red], tick=True)
                wslr[b].add_r(tlast)
            return tlast

        tlast = None
        for i in range(6):
            b = i % 2
            tl = kb.dma("sy", wsl[b][:, :], wred_d[i, :, :],
                        waits=wslr[b].wr(), sem="ws%d" % b)
            wslr[b].set_w(tl)
            t_red = kb.op("dv", lambda e, b=b: e.reduce_sum(
                sc[:, 15:16], wsl[b][:, :], axis=AX.X,
                apply_absolute_value=True), waits=[tl, t_mems], tick=True)
            tlast = kb.op("dv", lambda e, c=i // 2: e.tensor_add(
                accv[:, c:c + 1], accv[:, c:c + 1], sc[:, 15:16]),
                waits=[t_red], tick=True)
            wslr[b].add_r(tlast)
        t_rw = kb.dma("gp", rin_d[:, :], accv[:, :], waits=[tlast])
        kb.cnt["cc"] += 1
        rcv = kb.cnt["cc"]
        kb._wait("gp", t_rw)

        def run_rcc(e):
            e.collective_compute(
                "AllReduce", ALU.add, replica_groups=[list(range(8))],
                ins=[rin_d.ap().opt()], outs=[rout_d.ap().opt()],
            ).then_inc(kb.sems["cc"], 1)
        kb.steps["gp"].append(run_rcc)
        t_rcc = ("cc", rcv)
        t_rrd = kb.dma("gp", accv[:, :], rout_d[:, :], waits=[t_rcc])
        CT = {}
        onescol = sc[:, 14:15]
        t_of = kb.op("dv", lambda e: e.memset(onescol, 1.0), tick=True)
        t_sum = mm(pC[0:1, 8:12], onescol, accv[:, 0:4], True, True,
                   waits=[t_of, t_rrd] + pc_r.wr(), tick=True)
        pc_r.set_w(t_sum)
        t_s = kb.op("dv", lambda e: e.tensor_scalar(
            rows[0:1, 0:4], pC[0:1, 8:12], float(2.0 ** -24), 1e-8,
            ALU.mult, ALU.add), waits=[t_sum], tick=True)
        pc_r.add_r(t_s)
        t_bc = mm(pC[:, 12:16], onesr[:, :], rows[0:1, 0:4], True, True,
                  waits=[t_s] + pc_r.wr(), tick=True)
        pc_r.set_w(t_bc)
        t_scp = kb.op("dv", lambda e: e.tensor_copy(sc[:, 0:4], pC[:, 12:16]),
                      waits=[t_bc], tick=True)
        pc_r.add_r(t_scp)
        kb.op("dv", lambda e: e.tensor_scalar_mul(sc[:, 4:7], sc[:, 0:3], 0.5),
              waits=[t_scp])
        kb.op("dv", lambda e: e.tensor_scalar_mul(sc[:, 7:10], sc[:, 0:3], -0.5))
        t_g1 = kb.op("dv", lambda e: e.tensor_tensor(
            sc[:, 10:11], sc[:, 1:2], sc[:, 2:3], ALU.mult), tick=True)
        t_scal = kb.op("dv", lambda e: e.tensor_scalar_mul(
            sc[:, 10:11], sc[:, 10:11], 0.5), waits=[t_g1], tick=True)
        dbg("sc", sc[:, :], [t_scal])

        t8r = [R("t8o0"), R("t8o1")]

        def quant_mat(dram, t8dram, nslab, width, tc, mc, dn):
            for i in range(nslab):
                b = i % 2
                t8b = i % 2
                t8o = t8o_v2[t8b]
                tl = kb.dma("sy", wsl[b][:, 0:width],
                            dram[i * 128:(i + 1) * 128, :],
                            waits=wslr[b].wr(), sem="ws%d" % b)
                wslr[b].set_w(tl)
                t2 = None
                for hf in range(max(1, width // 4096)):
                    w0 = hf * 4096
                    wd = min(4096, width)
                    t_lt = kb.op("dv", lambda e, b=b, w0=w0, wd=wd, mc=mc:
                                 e.tensor_scalar(ltb_v[:, 0:wd],
                                                 wsl[b][:, w0:w0 + wd],
                                                 sc[:, mc:mc + 1], None,
                                                 ALU.is_lt),
                                 waits=[tl, t_scal] +
                                 (t8r[t8b].wr() if hf == 0 else []),
                                 tick=True)
                    t2 = kb.op("dv", lambda e, t8o=t8o, b=b, w0=w0, wd=wd,
                               tc=tc: e.scalar_tensor_tensor(
                                   t8o[:, w0:w0 + wd],
                                   wsl[b][:, w0:w0 + wd],
                                   sc[:, tc:tc + 1], ltb_v[:, 0:wd],
                                   ALU.is_gt, ALU.subtract), waits=[t_lt],
                               tick=True)
                    wslr[b].add_r(t2)
                t8r[t8b].set_w(t2)
                if not dn:
                    tw = kb.dma("sy",
                                t8dram[:, :, i * ESLAB:(i + 1) * ESLAB]
                                .rearrange("s p e -> p s e"),
                                t8o[:, 0:8192].rearrange(
                                    "p (s e) -> p s e", s=NSLAB),
                                waits=[t2], sem="t8%d" % t8b)
                    t8r[t8b].add_r(tw)
                else:
                    tw = kb.dma("sy", t8dram[:, :, i * 128:(i + 1) * 128]
                                .rearrange("m p e -> p m e"),
                                t8o[:, 0:2048].rearrange(
                                    "p (m e) -> p m e", m=16), waits=[t2])
                    t8r[t8b].add_r(tw)

        quant_mat(wupt_d, t8up_d, 16, 8192, 4, 7, False)
        G_UP = [("t80", kb.cnt.get("t80", 0)), ("t81", kb.cnt.get("t81", 0))]
        quant_mat(wgat_d, t8ga_d, 16, 8192, 5, 8, False)
        G_GA = [("t80", kb.cnt.get("t80", 0)), ("t81", kb.cnt.get("t81", 0))]

        tb16_2 = [bv(A5, 128, 4096), bv(A5 + 8192, 128, 4096)]
        stg8 = v8(A3, 128, 8192)
        stg_r = R("stg8")
        tb_rs = [R("tb16a"), R("tb16b")]
        for i in range(16):
            b = i % 2
            tl = kb.dma("sy", wsl[b][:, 0:8192],
                        wdnn_d[i * 128:(i + 1) * 128, :],
                        waits=wslr[b].wr(), sem="ws%d" % b)
            wslr[b].set_w(tl)
            for hf in range(2):
                w0 = hf * 4096
                tb16 = tb16_2[hf]
                tb_r = tb_rs[hf]
                t_lt = kb.op("dv", lambda e, b=b, w0=w0:
                             e.tensor_scalar(ltb_v[:, 0:4096],
                                             wsl[b][:, w0:w0 + 4096],
                                             sc[:, 9:10], None, ALU.is_lt),
                             waits=[tl, t_scal] + tb_r.wr(), tick=True)
                t2 = kb.op("dv", lambda e, tb16=tb16, b=b, w0=w0:
                           e.scalar_tensor_tensor(
                               tb16[:, :], wsl[b][:, w0:w0 + 4096],
                               sc[:, 6:7], ltb_v[:, 0:4096],
                               ALU.is_gt, ALU.subtract), waits=[t_lt],
                           tick=True)
                wslr[b].add_r(t2)
                tb_r.set_w(t2)
                for kg in range(8):
                    bk = kg % 2
                    for kk in range(4):
                        k = kg * 4 + kk
                        tmm = mm(pO[:, bk * 512 + kk * 128:
                                    bk * 512 + (kk + 1) * 128],
                                 tb16[:, k * 128:(k + 1) * 128], id_t[:, :],
                                 True, True,
                                 waits=(po[bk].wr() + [t2, t_id])
                                 if kk == 0 else (), tick=(kk == 3))
                    po[bk].set_w(tmm)
                    tb_r.add_r(tmm)
                    t_ev = act(stg8[:, (hf * 32 + kg * 4) * 128:
                                    (hf * 32 + kg * 4 + 4) * 128],
                               pO[:, bk * 512:(bk + 1) * 512], AF.Copy,
                               waits=[tmm] + (stg_r.wr()
                                              if (hf == 0 and kg == 0)
                                              else []))
                    po[bk].add_r(t_ev)
                    stg_r.set_w(t_ev) if (hf == 1 and kg == 7) else                         stg_r.add_r(t_ev)
            tw = kb.dma("sy", t8dn_d[i, :, :], stg8[:, :], waits=[t_ev],
                        sem="t84")
            stg_r.add_r(tw)
        G_DN = [("t84", kb.cnt.get("t84", 0))]
        t_q8 = kb.tok("sy")

        def emit_ln(pf):
            xr, xnr = R("x"), R("xn")
            tc = None
            w0 = list(pf) + xr.rd() + R("xb").wr()
            for c in range(KC):
                tc = kb.op("dv", lambda e, c=c: e.tensor_copy(
                    xb_v[:, c * 512:(c + 1) * 512],
                    x_t[:, c * 512:(c + 1) * 512]),
                    waits=(w0 if c == 0 else ()), tick=(c == KC - 1))
            R("xb").set_w(tc)
            tsq = kb.op("dv", lambda e: e.tensor_tensor(
                xsq_v[:, :], xb_v[:, :], xb_v[:, :], ALU.mult),
                waits=R("xsq").wr(), tick=True)
            R("xsq").set_w(tsq)
            for c in range(KC):
                mm(pC[0:1, 0:512], onesb[:, :], xb_v[:, c * 512:(c + 1) * 512],
                   c == 0, c == KC - 1, waits=(pc_r.wr() + [tc]) if c == 0
                   else ())
            for c in range(KC):
                tS = mm(pO[0:1, 0:512], onesb[:, :],
                        xsq_v[:, c * 512:(c + 1) * 512], c == 0, c == KC - 1,
                        waits=(po_wr() + [tsq]) if c == 0 else (),
                        tick=(c == KC - 1))
            pc_r.set_w(tS)
            po[0].set_w(tS)
            t_m = kb.op("dv", lambda e: e.tensor_scalar_mul(
                rows[0:1, 0:512], pC[0:1, 0:512], 1.0 / 2048.0), waits=[tS],
                tick=True)
            t_rd = kb.op("dv", lambda e: e.tensor_scalar_mul(
                rows[0:1, 512:1024], pO[0:1, 0:512], 1.0 / 2048.0), tick=True)
            pc_r.add_r(t_rd)
            po[0].add_r(t_rd)
            t_m2 = kb.op("dv", lambda e: e.tensor_tensor(
                rows[0:1, 1024:1536], rows[0:1, 0:512], rows[0:1, 0:512],
                ALU.mult), waits=[t_m], tick=True)
            t_var = kb.op("dv", lambda e: e.tensor_sub(
                rows[0:1, 512:1024], rows[0:1, 512:1024],
                rows[0:1, 1024:1536]), waits=[t_rd, t_m2], tick=True)
            t_ln = act(rows[0:1, 1024:1536], rows[0:1, 512:1024], AF.Ln,
                       waits=[t_var], bias=1e-5)
            t_r = act(rows[0:1, 512:1024], rows[0:1, 1024:1536], AF.Exp,
                      waits=[t_ln], scale=-0.5)
            t_mr = kb.op("dv", lambda e: e.tensor_tensor(
                rows[0:1, 1024:1536], rows[0:1, 0:512], rows[0:1, 512:1024],
                ALU.mult), waits=[t_r, t_m], tick=True)
            mm(pO[:, 0:512], onesr[:, :], rows[0:1, 512:1024], True, True,
               waits=po_wr() + [t_r, t_mr])
            t_bc2 = mm(pO[:, 512:1024], onesr[:, :], rows[0:1, 1024:1536],
                       True, True, tick=True)
            po[0].set_w(t_bc2)
            po[1].set_w(t_bc2)
            t_xn = None
            xn_toks = []
            for c in range(KC):
                t_t1 = kb.op("dv", lambda e, c=c: e.tensor_tensor(
                    t1_v[:, :], x_t[:, c * 512:(c + 1) * 512], pO[:, 0:512],
                    ALU.mult),
                    waits=(([t_bc2] + xnr.wr() + R("t1").wr()) if c == 0
                           else ()), tick=True)
                t_xn = kb.op("dv", lambda e, c=c: e.tensor_sub(
                    xn_t[:, c * 512:(c + 1) * 512], t1_v[:, :],
                    pO[:, 512:1024]), waits=[t_t1], tick=True)
                xn_toks.append(t_xn)
            po[0].add_r(t_xn)
            po[1].add_r(t_xn)
            xnr.set_w(t_xn)
            R("t1").set_w(t_xn)
            return t_xn, xn_toks, t_rd

        def emit_ffn(t_xn, pf, xn_toks=None, t_stats=None, q="sy",
                     gu=None, gg=None, gd=None, mid=None):
            xr = R("x")
            kb.out_toks = []
            wu = [R("wu0"), R("wu1")]
            wg = R("wg")
            hr, sr = R("h"), R("silu")
            t_h = None
            for s in range(NSLAB):
                b = s % 2
                tlu = kb.dma(q, wupb[b][:, :], t8up_d[s, :, :],
                             waits=wu[b].wr() +
                             ((gu if gu is not None else [t_q8]) + list(pf)
                              if s < 2 else []),
                             sem="wu%d" % b)
                wu[b].set_w(tlu)
                tlg = kb.dma(q, wgab[:, :], t8ga_d[s, :, :],
                             waits=wg.wr() +
                             ((gg if gg is not None else [t_q8]) +
                              [t_stats or t_xn] + list(pf)
                              if s == 0 else []),
                             sem="wgx")
                wg.set_w(tlg)
                silu_toks = []
                for m in range(8):
                    bk = m % 4
                    psl = pBig[:, bk * 512:(bk + 1) * 512]
                    for k in range(KC):
                        xw = [xn_toks[k]] if xn_toks else [t_xn]
                        tmm = mm(psl, wupb[b][:, k * ESLAB + m * 128:
                                              k * ESLAB + (m + 1) * 128],
                                 xn_t[:, k * 512:(k + 1) * 512],
                                 k == 0, k == KC - 1,
                                 waits=(pb[bk].wr() + [tlu] + xw)
                                 if k == 0 else xw, tick=(k == KC - 1))
                    pb[bk].set_w(tmm)
                    wu[b].add_r(tmm)
                    tsl = act(silu_v[:, m * 512:(m + 1) * 512], psl, AF.Silu,
                              waits=[tmm, t_scal] + (sr.wr() if m == 0
                                                     else []), scale=sc[:, 0:1])
                    pb[bk].add_r(tsl)
                    silu_toks.append(tsl)
                sr.set_w(silu_toks[-1])
                for m in range(8):
                    bk = m % 2
                    psl = pO[:, bk * 512:(bk + 1) * 512]
                    for k in range(KC):
                        xw = [xn_toks[k]] if xn_toks else [t_xn]
                        tmm = mm(psl, wgab[:, k * ESLAB + m * 128:
                                          k * ESLAB + (m + 1) * 128],
                                 xn_t[:, k * 512:(k + 1) * 512],
                                 k == 0, k == KC - 1,
                                 waits=(po[bk].wr() + [tlg] + xw)
                                 if k == 0 else xw, tick=(k == KC - 1))
                    po[bk].set_w(tmm)
                    wg.add_r(tmm)
                    t_h = kb.op("dv", lambda e, s=s, m=m, psl=psl:
                                e.tensor_tensor(
                                    h_v[:, (s * 8 + m) * 512:
                                        (s * 8 + m + 1) * 512],
                                    silu_v[:, m * 512:(m + 1) * 512], psl,
                                    ALU.mult),
                                waits=[tmm, silu_toks[m]] +
                                (hr.wr() if (s == 0 and m == 0) else []),
                                tick=True)
                    po[bk].add_r(t_h)
                    sr.add_r(t_h)
            hr.set_w(t_h)
            dbg("h_t%d" % kb.cnt.get("_ffn", 0), h_v, [t_h], hr)
            kb.cnt["_ffn"] = kb.cnt.get("_ffn", 0) + 1
            wd = [R("wd0"), R("wd1")]
            t_out = None
            t_gate_done = kb.tok("pe")
            if mid is not None:
                mid()
            for m in range(16):
                b = m % 2
                tld = kb.dma(q, wdnb[b][:, :], t8dn_d[m, :, :],
                             waits=wd[b].wr() +
                             ((gd if gd is not None else [t_q8]) +
                              [t_gate_done, t_h] if m < 2 else []),
                             sem="wd%d" % b)
                wd[b].set_w(tld)
                bk = m % 4
                psl = pBig[:, bk * 512:(bk + 1) * 512]
                for k in range(E // 128):
                    tmm = mm(psl, wdnb[b][:, k * 128:(k + 1) * 128],
                             h_v[:, k * 512:(k + 1) * 512],
                             k == 0, k == E // 128 - 1,
                             waits=(pb[bk].wr() + [tld, t_h]) if k == 0
                             else (), tick=(k == E // 128 - 1))
                pb[bk].set_w(tmm)
                wd[b].add_r(tmm)
                t_out = kb.op("dv", lambda e, m=m, psl=psl:
                              e.scalar_tensor_tensor(
                                  x_t[:, m * 512:(m + 1) * 512], psl,
                                  sc[:, 10:11], x_t[:, m * 512:(m + 1) * 512],
                                  ALU.mult, ALU.add),
                              waits=[tmm] + (xr.wr() if m == 0 else []),
                              tick=True)
                pb[bk].add_r(t_out)
                hr.add_r(t_out)
                kb.out_toks.append(t_out)
            xr.set_w(t_out)
            return t_out

        war = [R("wa0"), R("wa1"), R("wa2"), R("wa3")]

        def emit_proj(wdram, t_xn, evac, pf, xn_toks=None):
            t_ev = None
            for m in range(16):
                b = m % 4
                tl = kb.dma("sy", wab[b][:, :], wdram[m, :, :],
                            waits=war[b].wr() + (list(pf) if m < 4 else []),
                            sem="wa%d" % b)
                war[b].set_w(tl)
                bk = m % 2
                psl = pO[:, bk * 512:(bk + 1) * 512]
                for k in range(KC):
                    xw = [xn_toks[k]] if xn_toks else [t_xn]
                    tmm = mm(psl, wab[b][:, k * 128:(k + 1) * 128],
                             xn_t[:, k * 512:(k + 1) * 512], k == 0,
                             k == KC - 1,
                             waits=(po[bk].wr() + [tl] + xw) if k == 0
                             else xw, tick=(k == KC - 1))
                po[bk].set_w(tmm)
                war[b].add_r(tmm)
                t_ev = evac(m, psl, tmm)
                po[bk].add_r(t_ev)
            return t_ev

        def emit_gk(tile, g, gb, t_xn, xn_toks=None):
            gkr = R("gkg%d" % gb)
            for k in range(KC):
                xw = [xn_toks[k]] if xn_toks else [t_xn]
                tz = mm(pC[0:16, 0:512], wg1_t[:, k * 16:(k + 1) * 16],
                        xn_t[:, k * 512:(k + 1) * 512], k == 0, k == KC - 1,
                        waits=(pc_r.wr() + [t_wg1] + xw) if k == 0 else xw,
                        tick=(k == KC - 1))
            pc_r.set_w(tz)
            tms = kb.op("dv", lambda e: e.memset(z1_v[0:32, :], 1.0),
                        waits=R("z1").wr(), tick=True)
            tcp = kb.op("dv", lambda e: e.tensor_copy(
                z1_v[0:16, :], pC[0:16, 0:512]), waits=[tz, tms], tick=True)
            R("z1").set_w(tcp)
            pc_r.add_r(tcp)
            t_gk = None
            for hf in range(2):
                for n in range(2):
                    bk = n
                    tzz = mm(pO[:, bk * 512:(bk + 1) * 512],
                             z1_v[0:17, g * 128:(g + 1) * 128],
                             wg2_t[0:17, (hf * 2 + n) * 512:
                                   (hf * 2 + n + 1) * 512], True, True,
                             waits=(po[bk].wr() + [tcp, t_wg2]),
                             tick=True)
                    po[bk].set_w(tzz)
                t_a = act(zb1_v[:, 0:1024], pO[:, :], AF.Abs,
                          waits=[tzz] + R("zb1").wr())
                t_e = act(zb2_v[:, 0:1024], zb1_v[:, 0:1024], AF.Exp,
                          waits=[t_a] + R("zb2").wr(), scale=-1.0)
                t_l = act(zb1_v[:, 0:1024], zb2_v[:, 0:1024], AF.Ln,
                          waits=[t_e], bias=1.0)
                R("zb1").set_w(t_l)
                t_mn = kb.op("dv", lambda e: e.tensor_scalar_min(
                    zb2_v[:, 0:1024], pO[:, :], 0.0),
                    waits=[tzz, t_e] + (gkr.wr() if hf == 0 else []),
                    tick=True)
                R("zb2").set_w(t_mn)
                po[0].add_r(t_mn)
                po[1].add_r(t_mn)
                t_gk = kb.op("dv", lambda e, gb=gb, hf=hf: e.tensor_sub(
                    gkg_v[gb][:, hf * 1024:(hf + 1) * 1024],
                    zb2_v[:, 0:1024], zb1_v[:, 0:1024]),
                    waits=[t_l, t_mn], tick=True)
                R("zb1").add_r(t_gk)
                R("zb2").add_r(t_gk)
            gkr.set_w(t_gk)
            tw = kb.dma("gp", gk_d[tile * 4 + g, :, :], gkg_v[gb][:, :],
                        waits=[t_gk])
            gkr.add_r(tw)
            return t_gk

        vcr = [R("vc0"), R("vc1")]

        def emit_scan(tile, g, passb, t_kf, t_q, gk_ap, gk_reg,
                      Ktil_v, ATm_v, vsrc=None, vtok=None, split=False):
            glob_g = tile * NG + g
            vb = g % 2
            if vsrc is None:
                vC_v = vC_2[vb]
                tv = kb.dma("gp", vC_v[:, :], v_d[glob_g, :, :],
                            waits=vcr[vb].wr(), sem="vc%d" % vb)
                vcr[vb].set_w(tv)
            else:
                vC_v = vsrc
                tv = vtok
            t_gk = gk_reg.w

            for h in range(H):
                tg = mm(pBig[:, h * 128:(h + 1) * 128],
                        gk_ap[:, h * 128:(h + 1) * 128], u2_t[:, :],
                        True, True,
                        waits=(pbig_wr() + [t_gk, t_u2]) if h == 0 else (),
                        tick=(h == H - 1))
            for b in pb:
                b.set_w(tg)
            gl_view = pBig.ap().rearrange(
                "p (h l) -> p h l", h=H)[:, :, 127:128]
            t_egl = act(egl_f[:, 0:16].unsqueeze(2), gl_view, AF.Exp,
                        waits=[tg] + R("egl").wr(), scale=G16)
            R("egl").set_w(t_egl)
            for b in pb:
                b.add_r(t_egl)
            if passb:
                t_eqg = act(Eqg_v[:, :], pBig[:, :], AF.Exp,
                            waits=R("Eqg").wr(), scale=G16)
                R("Eqg").set_w(t_eqg)
                for b in pb:
                    b.add_r(t_eqg)
            else:
                t_gt = kb.op("dv", lambda e: e.tensor_add(
                    Gtot[:, :], Gtot[:, :], gl_view.squeeze(2)),
                    waits=[tg], tick=True)
                for b in pb:
                    b.add_r(t_gt)
            if passb:
                for h in range(H):
                    tg2 = mm(pBig[:, h * 128:(h + 1) * 128],
                             gk_ap[:, h * 128:(h + 1) * 128], u2x_t[:, :],
                             True, True,
                             waits=(pbig_wr() + [t_u2x]) if h == 0 else (),
                             tick=(h == H - 1))
                for b in pb:
                    b.set_w(tg2)
                t_eqh = act(EQh_v[:, :], pBig[:, :], AF.Exp,
                            waits=[tg2] + R("EQh").wr(), scale=G16)
                t_ekh = act(EKh_v[:, :], pBig[:, :], AF.Exp,
                            waits=R("EKh").wr(), scale=-G16)
                R("EQh").set_w(t_eqh)
                R("EKh").set_w(t_ekh)
                for b in pb:
                    b.add_r(t_ekh)
            for n in range(4):
                tr = mm(pBig[:, n * 512:(n + 1) * 512], v2_t[:, :],
                        gk_ap[:, n * 512:(n + 1) * 512], True, True,
                        waits=(pbig_wr() + [t_v2]) if n == 0 else (),
                        tick=(n == 3))
            for b in pb:
                b.set_w(tr)
            t_eks = act(EKs_v[:, :], pBig[:, :], AF.Exp,
                        waits=[tr] + R("EKs").wr(), scale=G16)
            R("EKs").set_w(t_eks)
            for b in pb:
                b.add_r(t_eks)
            gk_reg.add_r(t_eks)

            if passb:
                qsl = e3(q_v)[:, :, g * 128:(g + 1) * 128]
                ksl = e3(kF_v)[:, :, g * 128:(g + 1) * 128]
                t_qg = kb.op("dv", lambda e, qsl=qsl: e.tensor_tensor(
                    e3(qg_v), qsl, e3(Eqg_v), ALU.mult),
                    waits=[R("Eqg").w, t_q] + R("qg").wr(), tick=True)
                t_qh = kb.op("dv", lambda e, qsl=qsl: e.tensor_tensor(
                    e3(Qh_v), qsl, e3(EQh_v), ALU.mult),
                    waits=[t_eqh] + R("Qh").wr(), tick=True)
                t_kh = kb.op("dv", lambda e, ksl=ksl: e.tensor_tensor(
                    e3(Kh_v), ksl, e3(EKh_v), ALU.mult),
                    waits=[t_ekh, t_kf] + R("Kh").wr(), tick=True)
                R("qg").set_w(t_qg)
                R("Qh").set_w(t_qh)
                R("Kh").set_w(t_kh)
                R("Eqg").add_r(t_qg)
                R("EQh").add_r(t_qh)
                R("EKh").add_r(t_kh)

            for half in range(2):
                for hh in range(8):
                    h = half * 8 + hh
                    tt = kb.op("pe", lambda e, h=h, hh=hh: e.transpose(
                        pD[:, hh * 128:(hh + 1) * 128],
                        kF_v[:, h * 512 + g * 128: h * 512 + (g + 1) * 128],
                        id_t[:, :]),
                        waits=(pd_r.wr() + [t_kf, t_id]) if hh == 0 else (),
                        tick=(hh == 7))
                pd_r.set_w(tt)
                t_kt = kb.op("dv", lambda e, half=half: e.tensor_tensor(
                    Ktil_v[:, half * 1024:(half + 1) * 1024], pD[:, :],
                    EKs_v[:, half * 1024:(half + 1) * 1024], ALU.mult),
                    waits=[tt, t_eks] + (R("Ktil").wr() if half == 0 else []),
                    tick=True)
                pd_r.add_r(t_kt)
                R("Ktil").set_w(t_kt)
                R("EKs").add_r(t_kt)
            t_ktil = R("Ktil").w

            if passb:
                for r4 in range(4):
                    for hh in range(4):
                        h = r4 * 4 + hh
                        tA = mm(pC[:, hh * 128:(hh + 1) * 128],
                                Kh_v[:, h * 128:(h + 1) * 128],
                                Qh_v[:, h * 128:(h + 1) * 128], True, True,
                                waits=(pc_r.wr() + [R("Kh").w, R("Qh").w])
                                if hh == 0 else (), tick=(hh == 3))
                    pc_r.set_w(tA)
                    t_msk = kb.op("dv", lambda e, r4=r4: e.tensor_tensor(
                        ATm_v[:, r4 * 512:(r4 + 1) * 512], pC[:, :],
                        am8_t[:, :], ALU.mult),
                        waits=[tA, t_am] + (R("ATm").wr() if r4 == 0 else []),
                        tick=True)
                    pc_r.add_r(t_msk)
                    R("ATm").set_w(t_msk)
                t_atm = R("ATm").w
                for h in range(H):
                    mm(pBig[:, h * 128:(h + 1) * 128],
                       S_b[:, h * 128:(h + 1) * 128],
                       qg_v[:, h * 128:(h + 1) * 128], True, False,
                       waits=(pbig_wr() + [R("qg").w, R("S_b").w, t_atm, tv])
                       if h == 0 else ())
                    t_o = mm(pBig[:, h * 128:(h + 1) * 128],
                             vC_v[:, h * 128:(h + 1) * 128],
                             ATm_v[:, h * 128:(h + 1) * 128], False, True,
                             tick=(h == H - 1))
                for b in pb:
                    b.set_w(t_o)
                R("S_b").add_r(t_o)
                R("ATm").add_r(t_o)
                R("qg").add_r(t_o)
                if vsrc is None:
                    vcr[vb].add_r(t_o)
                t_sq = act(o2_v[:, :], pBig[:, :], AF.Square,
                           waits=[t_o] + R("o2").wr())
                R("o2").set_w(t_sq)
                for b in pb:
                    b.add_r(t_sq)
                for r4 in range(4):
                    t_cs = mm(pC[0:1, 0:512], onesb[:, :],
                              o2_v[:, r4 * 512:(r4 + 1) * 512], True, True,
                              waits=pc_r.wr() + [t_sq], tick=True)
                    pc_r.set_w(t_cs)
                    t_lnr = act(rows[0:1, 0:512], pC[0:1, 0:512], AF.Ln,
                                waits=[t_cs], scale=1.0 / 128.0, bias=1e-5)
                    pc_r.add_r(t_lnr)
                    t_rr = act(rows2[0:1, r4 * 512:(r4 + 1) * 512],
                               rows[0:1, 0:512], AF.Exp,
                               waits=[t_lnr] + (R("rows2").wr() if r4 == 0
                                                else []), scale=-0.5)
                    R("rows2").set_w(t_rr)
                t_rrow = R("rows2").w
                gview = e3(gs_v)[:, :, g * 128:(g + 1) * 128]
                t_og1 = kb.op("dv", lambda e, gview=gview: e.tensor_tensor(
                    e3(og1_v), e3(pBig.ap()), gview, ALU.mult),
                    waits=[t_o, t_sq, R("gs").w] + R("og1").wr(), tick=True)
                R("og1").set_w(t_og1)
                for b in pb:
                    b.add_r(t_og1)
                t_og2 = None
                for r4 in range(4):
                    bk = r4 % 2
                    t_rb = mm(pO[:, bk * 512:(bk + 1) * 512], nw_t[0:1, :],
                              rows2[0:1, r4 * 512:(r4 + 1) * 512], True, True,
                              waits=(po[bk].wr() + [t_rrow, t_nw]),
                              tick=True)
                    po[bk].set_w(t_rb)
                    t_og2 = kb.op("dv", lambda e, r4=r4, bk=bk:
                                  e.tensor_tensor(
                                      e3(og_v)[:, r4 * 4:(r4 + 1) * 4,
                                               g * 128:(g + 1) * 128],
                                      e3(og1_v)[:, r4 * 4:(r4 + 1) * 4, :],
                                      pO[:, bk * 512:(bk + 1) * 512]
                                      .rearrange("p (h l) -> p h l", h=4),
                                      ALU.mult),
                                  waits=[t_rb, t_og1] +
                                  (R("og").wr() if r4 == 0 else []),
                                  tick=True)
                    po[bk].add_r(t_og2)
                R("og").set_w(t_og2)
            if split:
                yield
            for h in range(H):
                t_kv = mm(pBig[:, h * 128:(h + 1) * 128],
                          Ktil_v[:, h * 128:(h + 1) * 128],
                          vC_v[:, h * 128:(h + 1) * 128], True, True,
                          waits=(pbig_wr() + [t_ktil, tv]) if h == 0
                          else (), tick=(h == H - 1))
            for b in pb:
                b.set_w(t_kv)
            R("Ktil").add_r(t_kv)
            if vsrc is None:
                vcr[vb].add_r(t_kv)
            else:
                R("vAll").add_r(t_kv)
            t_sdec = None
            for h in range(H):
                t_sdec = kb.op("dv", lambda e, h=h: e.tensor_scalar(
                    S_f[:, h * 128:(h + 1) * 128],
                    S_f[:, h * 128:(h + 1) * 128],
                    egl_f[:, h:h + 1], None, ALU.mult),
                    waits=(([R("egl").w] + R("S_f").wr()) if h == 0 else ()),
                    tick=(h == H - 1))
            t_sadd = kb.op("dv", lambda e: e.tensor_add(
                S_f[:, :], S_f[:, :], pBig.ap()), waits=[t_kv, t_sdec],
                tick=True)
            R("S_f").set_w(t_sadd)
            R("egl").add_r(t_sdec)
            for b in pb:
                b.add_r(t_sadd)
            if passb:
                t_sb = kb.op("dv", lambda e: e.tensor_copy(
                    S_b[:, :], S_f[:, :]), waits=R("S_b").wr() + [t_sadd],
                    tick=True)
                R("S_b").set_w(t_sb)

        x_reg, xn_reg = R("x"), R("xn")
        for tile in range(NT):
            pf = fence()
            if tile >= 1:
                wnm, srcd, dstd = (("q", wq_d, wqb_d), ("g", wgt_d, wgtb_d),
                                   ("o", wo_d, wob_d))[tile - 1]
                CT[wnm] = kb.dma("gp", dstd[:, :, :], srcd[:, :, :],
                                 sem="cb" + wnm)
            tx = xpre
            if tile > 0:
                for c in range(KC):
                    tx = kb.dma("sy", x_t[:, c * 512:(c + 1) * 512],
                                x_d[c * 128:(c + 1) * 128,
                                    tile * 512:(tile + 1) * 512],
                                waits=(x_reg.wr() + pf) if c == 0 else ())
            x_reg.set_w(tx)
            dbg("xld_t%d" % tile, x_t, [tx], x_reg)
            t_xn, xtk, t_st = emit_ln(pf)
            dbg("xn_t%d" % tile, xn_t, [t_xn])
            if tile == 0:
                def mid_casts():
                    for wnm, srcd, dstd in (("k", wk_d, wkb_d),
                                            ("v", wv_d, wvb_d)):
                        CT[wnm] = kb.dma("gp", dstd[:, :, :], srcd[:, :, :],
                                         sem="cb" + wnm)
                t_f1 = emit_ffn(t_xn, pf, xtk, t_st, q="gp", gu=G_UP,
                                gg=G_GA, gd=G_DN, mid=mid_casts)
            else:
                t_f1 = emit_ffn(t_xn, pf, xtk, t_st)
            dbg("x1_t%d" % tile, x_t, [t_f1], x_reg)
            t_xn1, xtk1, _ts1 = emit_ln([])
            dbg("xn1_t%d" % tile, xn_t, [t_xn1], xn_reg)
            def ev_k(m, psl, tmm, tile=tile):
                t = kb.op("dv", lambda e, m=m, psl=psl: e.tensor_copy(
                    kF_v[:, m * 512:(m + 1) * 512], psl),
                    waits=[tmm] + (R("kF").wr() if m == 0 else []), tick=True)
                R("kF").set_w(t)
                tw = kb.dma("gp", k_d[m, :, tile * 512:(tile + 1) * 512],
                            kF_v[:, m * 512:(m + 1) * 512], waits=[t])
                R("kF").add_r(tw)
                return t

            emit_proj(wkb_d, t_xn1, ev_k, fence() + [CT["k"]], xtk1)
            t_kf = R("kF").w
            f1_toks = list(kb.out_toks)
            for c in range(KC):
                twx = kb.dma("gp", x1_d[c, :, tile * 512:(tile + 1) * 512],
                             x_t[:, c * 512:(c + 1) * 512],
                             waits=[f1_toks[c]])
                x_reg.add_r(twx)
                twn = kb.dma("gp", xn_d[c, :, tile * 512:(tile + 1) * 512],
                             xn_t[:, c * 512:(c + 1) * 512],
                             waits=[xtk1[c]])
                xn_reg.add_r(twn)

            vr = R("vAll")
            for m in range(16):
                b = m % 4
                tl = kb.dma("sy", wab[b][:, :], wvb_d[m, :, :],
                            waits=war[b].wr() + (fence() + [CT["v"]]
                                                 if m < 4 else []),
                            sem="wa%d" % b)
                war[b].set_w(tl)
                bk = m % 2
                for t4 in range(4):
                    psl = pO[:, bk * 512 + t4 * 128:
                             bk * 512 + (t4 + 1) * 128]
                    for k in range(KC):
                        xw = [xtk1[k]]
                        tmm = mm(psl,
                                 xn_t[:, k * 512 + t4 * 128:
                                      k * 512 + (t4 + 1) * 128],
                                 wab[b][:, k * 128:(k + 1) * 128],
                                 k == 0, k == KC - 1,
                                 waits=(po[bk].wr() + [tl] + xw)
                                 if (t4 == 0 and k == 0) else xw,
                                 tick=(t4 == 3 and k == KC - 1))
                po[bk].set_w(tmm)
                war[b].add_r(tmm)
                tvv = kb.op("dv", lambda e, m=m, bk=bk: e.tensor_copy(
                    vAll_v.rearrange("p (a n) -> p a n", a=4)[
                        :, :, m * 128:(m + 1) * 128],
                    pO[:, bk * 512:(bk + 1) * 512].rearrange(
                        "p (a n) -> p a n", a=4)),
                    waits=[tmm] + (vr.wr() if m == 0 else []), tick=True)
                po[bk].add_r(tvv)
                vr.set_w(tvv)
            for t4 in range(4):
                tw = kb.dma("gp", v_d[tile * 4 + t4, :, :],
                            vAll_v[:, t4 * 2048:(t4 + 1) * 2048],
                            waits=[vr.w])
                vr.add_r(tw)

            dbg("vall0", vAll_v, [vr.w])
            emit_gk(tile, 0, 0, t_xn1, xtk1)
            for g in range(NG):
                gb = g % 2
                gen = emit_scan(tile, g, False, t_kf, None, gkg_v[gb],
                                R("gkg%d" % gb), KtilA_v, ATmA_v,
                                vsrc=vAll_v[:, g * 2048:(g + 1) * 2048],
                                vtok=vr.w, split=True)
                next(gen)
                if g + 1 < NG:
                    emit_gk(tile, g + 1, (g + 1) % 2, t_xn1, xtk1)
                for _ in gen:
                    pass

        def emit_combine():
            t_sS = kb.dma("gp", ccin_d[:, 0:2048], S_f[:, :],
                          waits=[R("S_f").w])
            t_sG = kb.dma("gp", ccin_d[:, 2048:2064], Gtot[:, :],
                          waits=[kb.tok("dv")])
            kb.cnt["cc"] += 1
            ccv = kb.cnt["cc"]
            for t in (t_sS, t_sG):
                kb._wait("gp", t)

            def run_cc(e):
                e.collective_compute(
                    "AllGather", ALU.bypass, replica_groups=[list(range(8))],
                    ins=[ccin_d.ap().opt()], outs=[ccout_d.ap().opt()],
                ).then_inc(kb.sems["cc"], 1)
            kb.steps["gp"].append(run_cc)
            t_cc = ("cc", ccv)

            pf = fence()
            t_gg = None
            for j in range(8):
                t_gg = kb.dma("gp", gG_v[:, j * 16:(j + 1) * 16],
                              ccout_d[j * 128:(j + 1) * 128, 2048:2064],
                              waits=[t_cc] + (pf if j == 0 else []))
            t_s0 = kb.op("dv", lambda e: e.memset(S0_v[:, :], 0.0),
                         waits=[t_cc] + pf, tick=True)
            t_acc = t_s0
            gsr = R("gS")
            for j in range(8):
                tSj = kb.dma("gp", gS_v[:, :],
                             ccout_d[j * 128:(j + 1) * 128, 0:2048],
                             waits=[t_cc] + gsr.wr())
                gsr.set_w(tSj)
                t_tj = kb.op("dv", lambda e, j=j: e.tensor_scalar(
                    tj_v[:, :], gG_v[:, 0:16], cm_t[:, j * 8: j * 8 + 1], None,
                    ALU.mult), waits=[t_gg, t_cm, t_s0], tick=True)
                for i in range(1, 8):
                    t_tj = kb.op("dv", lambda e, j=j, i=i: e.scalar_tensor_tensor(
                        tj_v[:, :], gG_v[:, i * 16:(i + 1) * 16],
                        cm_t[:, j * 8 + i: j * 8 + i + 1], tj_v[:, :],
                        ALU.mult, ALU.add), waits=[t_tj], tick=True)
                t_ej = act(ej_v[:, :], tj_v[:, :], AF.Exp, waits=[t_tj],
                           scale=G16)
                t_ejm = kb.op("dv", lambda e, j=j: e.tensor_scalar(
                    ej_v[:, :], ej_v[:, :], cm_t[:, 64 + j: 64 + j + 1], None,
                    ALU.mult), waits=[t_ej], tick=True)
                for h in range(H):
                    t_acc = kb.op("dv", lambda e, h=h: e.scalar_tensor_tensor(
                        S0_v[:, h * 128:(h + 1) * 128],
                        gS_v[:, h * 128:(h + 1) * 128],
                        ej_v[:, h:h + 1], S0_v[:, h * 128:(h + 1) * 128],
                        ALU.mult, ALU.add),
                        waits=[tSj, t_ejm, t_acc] if h == 0 else (),
                        tick=(h == H - 1))
                gsr.add_r(t_acc)
            t_scpS = kb.op("dv", lambda e: e.tensor_copy(S_f[:, :], S0_v[:, :]),
                           waits=[t_acc], tick=True)
            R("S_f").set_w(t_scpS)
            t_sbc = kb.op("dv", lambda e: e.tensor_copy(S_b[:, :], S_f[:, :]),
                          waits=R("S_b").wr() + [t_scpS], tick=True)
            R("S_b").set_w(t_sbc)
            dbg("S_A", Gtot[:, :], [kb.tok("dv")])
            dbg("S0", S_f[:, :], [R("S_f").w])


        for tile in range(NT):
            pf = fence()
            txn = None
            xn_ltoks = []
            for c in range(KC):
                txn = kb.dma("sy", xn_t[:, c * 512:(c + 1) * 512],
                             xn_d[c, :, tile * 512:(tile + 1) * 512],
                             waits=(xn_reg.wr() + pf) if c == 0 else ())
                xn_ltoks.append(txn)
            xn_reg.set_w(txn)
            t_kf = None
            for hM in range(H):
                t_kf = kb.dma("sy", kF_v[:, hM * 512:(hM + 1) * 512],
                              k_d[hM, :, tile * 512:(tile + 1) * 512],
                              waits=(R("kF").wr() + pf) if hM == 0 else ())
            R("kF").set_w(t_kf)

            def ev_q(m, psl, tmm):
                t = act(q_v[:, m * 512:(m + 1) * 512], psl, AF.Copy,
                        waits=[tmm] + (R("q").wr() if m == 0 else []),
                        scale=QSCALE)
                R("q").set_w(t)
                return t

            def ev_g(m, psl, tmm):
                t = act(gs_v[:, m * 512:(m + 1) * 512], psl, AF.Silu,
                        waits=[tmm] + (R("gs").wr() if m == 0 else []))
                R("gs").set_w(t)
                return t

            emit_proj(wqb_d, txn, ev_q, list(pf) + [CT["q"]])
            t_q = R("q").w
            emit_proj(wgtb_d, txn, ev_g, [CT["g"]])
            if tile == 0:
                emit_combine()
            for g in range(NG):
                gb = g % 2
                tgk = kb.dma("gp", gkg_v[gb][:, :], gk_d[tile * 4 + g, :, :],
                             waits=R("gkg%d" % gb).wr() +
                             (pf if g == 0 else []), sem="gkq%d" % gb)
                R("gkg%d" % gb).set_w(tgk)
                for _ in emit_scan(tile, g, True, t_kf, t_q, gkg_v[gb],
                                   R("gkg%d" % gb), KtilB_v, ATmB_v):
                    pass
            t_og = kb.tok("dv")
            dbg("og0", og_v, [t_og])
            tx = None
            for c in range(KC):
                tx = kb.dma("sy", x_t[:, c * 512:(c + 1) * 512],
                            x1_d[c, :, tile * 512:(tile + 1) * 512],
                            waits=(x_reg.wr() + [t_og]) if c == 0 else ())
            x_reg.set_w(tx)
            t_wo = None
            for m in range(16):
                b = m % 4
                tl = kb.dma("sy", wab[b][:, :], wob_d[m, :, :],
                            waits=war[b].wr() + [t_og, CT["o"]],
                            sem="wa%d" % b)
                war[b].set_w(tl)
                bk = m % 2
                psl = pO[:, bk * 512:(bk + 1) * 512]
                for k in range(KC):
                    tmm = mm(psl, wab[b][:, k * 128:(k + 1) * 128],
                             og_v[:, k * 512:(k + 1) * 512], k == 0,
                             k == KC - 1,
                             waits=(po[bk].wr() + [tl, t_og]) if k == 0
                             else (), tick=(k == KC - 1))
                po[bk].set_w(tmm)
                war[b].add_r(tmm)
                t_wo = kb.op("dv", lambda e, m=m, psl=psl: e.tensor_add(
                    x_t[:, m * 512:(m + 1) * 512],
                    x_t[:, m * 512:(m + 1) * 512], psl),
                    waits=[tmm, tx], tick=True)
                po[bk].add_r(t_wo)
                R("og").add_r(t_wo)
            x_reg.set_w(t_wo)
            dbg("xwo", x_t, [t_wo])
            t_xn2, xtk2, t_st2 = emit_ln(fence())
            t_f2 = emit_ffn(t_xn2, fence(), xtk2, t_st2)
            f2_toks = list(kb.out_toks)
            for c in range(KC):
                tw = kb.dma("sy", y_d[c * 128:(c + 1) * 128,
                                      tile * 512:(tile + 1) * 512],
                            x_t[:, c * 512:(c + 1) * 512],
                            waits=[f2_toks[c]])
                x_reg.add_r(tw)

        kb._wait("sy", kb.tok("gp"))
        kb._wait("sy", kb.tok("sy"))

        @block.tensor
        def _(eng):
            for f in kb.steps["pe"]:
                f(eng)

        @block.vector
        def _(eng):
            for f in kb.steps["dv"]:
                f(eng)

        @block.scalar
        def _(eng):
            for f in kb.steps["ac"]:
                f(eng)

        @block.sync
        def _(eng):
            for f in kb.steps["sy"]:
                f(eng)

        @block.gpsimd
        def _(eng):
            for f in kb.steps["gp"]:
                f(eng)

    return nc


def kernel(**inputs):
    in_maps = host_prep(inputs)
    nc = build()
    res = run_bass_kernel_spmd(nc, in_maps, core_ids=list(range(8)))
    out = np.empty((2, 8192, D), np.float32)
    for c in range(8):
        b, s = c // 4, c % 4
        out[b, s * TSPAN:(s + 1) * TSPAN, :] = res.results[c]["y"].T
    return out
